# version 12
# baseline (speedup 1.0000x reference)
import numpy as np
from contextlib import ExitStack
import concourse.bass as bass
import concourse.mybir as mybir
from concourse.bass_utils import run_bass_kernel_spmd

F32 = mybir.dt.float32
F16 = mybir.dt.float16
AF = mybir.ActivationFunctionType
ALU = mybir.AluOpType
AXX = mybir.AxisListType.X

NCORE = 8
T_ALL = 16384
D = 1024
FF = 4096
J = 8
JF = 32
TC = T_ALL // NCORE
L = 512
NST = TC // L
LF = 256
NSF = TC // LF
EPS = 1e-5
C_NHALF = 1


def build(debug=False):
    nc = bass.Bass("TRN2", target_bir_lowering=False)

    xs = nc.dram_tensor("xs", [D, TC + 2], F32, kind="ExternalInput").ap()
    wi = nc.dram_tensor("wi", [D, 2 * D], F32, kind="ExternalInput").ap()
    wo = nc.dram_tensor("wo", [D, D], F32, kind="ExternalInput").ap()
    w1n = nc.dram_tensor("w1n", [FF, D], F32, kind="ExternalInput").ap()
    w2t = nc.dram_tensor("w2t", [FF, D], F32, kind="ExternalInput").ap()
    vec_in = {}
    for nm in ("cw0", "cw1", "cw2", "cb", "la", "sb", "sc", "sd", "ob", "lnw", "lnb", "b2"):
        vec_in[nm] = nc.dram_tensor(nm, [128, J], F32, kind="ExternalInput").ap()
    ipb_d = nc.dram_tensor("ipb", [128, 16], F32, kind="ExternalInput").ap()
    b1_d = nc.dram_tensor("b1", [128, JF], F32, kind="ExternalInput").ap()
    cst_d = nc.dram_tensor("cst", [128, 8], F32, kind="ExternalInput").ap()
    sele_d = nc.dram_tensor("sele", [8, 1024], F32, kind="ExternalInput").ap()
    idt_d = nc.dram_tensor("idt", [128, 128], F32, kind="ExternalInput").ap()
    sel_d = nc.dram_tensor("sel", [128, J], F32, kind="ExternalInput").ap()
    msk_d = nc.dram_tensor("msk", [128, 2], F32, kind="ExternalInput").ap()

    out_d = nc.dram_tensor("out", [D, TC], F32, kind="ExternalOutput").ap()
    dbg = {}
    if debug:
        for nm, shp in (("yloc", [D, TC]), ("x2", [D, TC]), ("xn0", [D, L]),
                        ("vw0", [D, 514]), ("hin", [128, J])):
            dbg[nm] = nc.dram_tensor("dbg_" + nm, shp, F32, kind="ExternalOutput").ap()

    w1b_dram = nc.dram_tensor("w1b_s", [JF, 128, D], F16)
    w2t_dram = nc.dram_tensor("w2t_s", [FF, D], F16)
    cc_in = nc.dram_tensor("cc_in", [128, J], F32)
    cc_out = nc.dram_tensor("cc_out", [NCORE, 128, J], F32)

    ES = ExitStack()
    sem_names = ["qx", "qs", "qg", "cc", "eT", "eV", "eS", "eG",
                 "qwA", "qwB", "q2A", "q2B", "qaA", "qaB", "qmA", "qmB",
                 "qsA", "qsB", "q3A", "q3B", "q3C", "q3D",
                 "qtA", "qtB", "qtC", "qtD"]
    sems = {n: ES.enter_context(nc.semaphore(n)) for n in sem_names}
    cnt = {n: 0 for n in sem_names}

    def inc(ins, key, n=1):
        cnt[key] += n
        ins.then_inc(sems[key], n)
        return ins

    def dinc(ins, key):
        return inc(ins, key, 16)

    def wg(eng, key, val=None):
        v = cnt[key] if val is None else val
        if v > 0:
            eng.wait_ge(sems[key], v)

    def fill(v, out_ap, val):
        P, W = out_ap.shape[0], out_ap.shape[1]
        return v.tensor_scalar(out=out_ap, in0=CST[0:P, 0:1].to_broadcast([P, W]),
                               scalar1=0.0, scalar2=float(val), op0=ALU.mult, op1=ALU.add)

    g = {}
    for nm in ("cw0", "cw1", "cw2", "cb", "la", "sb", "sc", "sd", "ob", "lnw", "lnb", "b2"):
        g[nm] = ES.enter_context(nc.sbuf_tensor(nm + "_t", [128, J], F32))
    IPB = ES.enter_context(nc.sbuf_tensor("ipb_t", [128, 16], F32))
    B1 = ES.enter_context(nc.sbuf_tensor("b1_t", [128, JF], F32))
    CST = ES.enter_context(nc.sbuf_tensor("cst_t", [128, 8], F32))
    SELE = ES.enter_context(nc.sbuf_tensor("sele_t", [8, 1024], F32))
    IDT = ES.enter_context(nc.sbuf_tensor("idt_t", [128, 128], F32))
    SEL = ES.enter_context(nc.sbuf_tensor("sel_t", [128, J], F32))
    MSK = ES.enter_context(nc.sbuf_tensor("msk_t", [128, 2], F32))
    AV = ES.enter_context(nc.sbuf_tensor("av_t", [128, J], F32))
    Y = ES.enter_context(nc.sbuf_tensor("y_t", [128, J * TC], F32))
    SST = ES.enter_context(nc.sbuf_tensor("sst_t", [128, J], F32))
    VH = ES.enter_context(nc.sbuf_tensor("vh_t", [128, 2 * J], F32))
    SALL = ES.enter_context(nc.sbuf_tensor("sall_t", [128, NCORE * J], F32))
    PIN = ES.enter_context(nc.sbuf_tensor("pin_t", [128, J * NCORE], F32))
    HIN = ES.enter_context(nc.sbuf_tensor("hin_t", [128, J], F32))
    RR_ = ES.enter_context(nc.sbuf_tensor("rr_t", [128, J], F32))
    COEF = ES.enter_context(nc.sbuf_tensor("coef_t", [128, J], F32))
    A512 = ES.enter_context(nc.sbuf_tensor("a512_t", [128, J], F32))
    A2K = ES.enter_context(nc.sbuf_tensor("a2k_t", [128, J], F32))
    ST1R = ES.enter_context(nc.sbuf_tensor("st1r_t", [128, NST * 4], F32))
    ST1M = ES.enter_context(nc.sbuf_tensor("st1m_t", [128, NST * 4], F32))
    ACC1 = ES.enter_context(nc.sbuf_tensor("acc1_t", [128, 8], F32))
    ACC2 = ES.enter_context(nc.sbuf_tensor("acc2_t", [128, 8], F32))
    AS1 = ES.enter_context(nc.sbuf_tensor("as1_t", [128, 256], F32))
    THRW = ES.enter_context(nc.sbuf_tensor("thrw_t", [128, 4], F32))
    STAT = ES.enter_context(nc.sbuf_tensor("stat_t", [128, 16], F32))
    STK = ES.enter_context(nc.sbuf_tensor("stk_t", [128, 8], F32))
    STKT = ES.enter_context(nc.sbuf_tensor("stkt_t", [8, 128], F32))
    DUMP = ES.enter_context(nc.sbuf_tensor("dump_t", [128, 1024], F16))
    XPH = ES.enter_context(nc.sbuf_tensor("xph_t", [128, J * L], F16))
    XPL = ES.enter_context(nc.sbuf_tensor("xpl_t", [128, J * L], F16))
    XHH = ES.enter_context(nc.sbuf_tensor("xhh_t", [128, 2 * J], F16))
    XHL = ES.enter_context(nc.sbuf_tensor("xhl_t", [128, 2 * J], F16))
    XT = ES.enter_context(nc.sbuf_tensor("xt_t", [128, J * L], F32))
    SCA = ES.enter_context(nc.sbuf_tensor("sca_t", [128, J * L], F32))
    SCB = ES.enter_context(nc.sbuf_tensor("scb_t", [128, J * L], F32))
    SCR = ES.enter_context(nc.sbuf_tensor("scr_t", [128, 2 * L], F32))

    PS_T = ES.enter_context(nc.psum_tensor("ps_t", [128, 2048], F32))
    PS_BC = ES.enter_context(nc.psum_tensor("ps_bc", [128, 1024], F32))
    PS_MM = ES.enter_context(nc.psum_tensor("ps_mm", [128, 1024], F32))

    block = ES.enter_context(nc.Block())

    _DRAIN_OPS = frozenset((
        "tensor_copy", "tensor_tensor", "tensor_add", "tensor_sub", "tensor_mul",
        "tensor_max", "tensor_scalar", "tensor_scalar_add", "tensor_scalar_mul",
        "tensor_single_scalar", "scalar_tensor_tensor", "tensor_tensor_scan",
        "tensor_reduce", "reduce_sum", "reduce_max", "reciprocal", "select",
        "activation", "copy", "add", "mul", "bn_stats", "bn_aggr", "transpose",
        "iota"))
    _dw_state = {}

    def _ap_rng(a):
        try:
            t = a.tensor.name
            lo = int(a.offset)
            span = 1
            for st_, ct_ in list(a.ap)[1:]:
                span += abs(int(st_)) * (int(ct_) - 1)
            return (t, lo, lo + span)
        except Exception:
            return None

    def _overlap(r1, r2):
        return r1[0] == r2[0] and r1[1] < r2[2] and r2[1] < r1[2]

    class _DW:
        def __init__(self, e, st):
            object.__setattr__(self, "_e", e)
            object.__setattr__(self, "_st", st)
        def __getattr__(self, n):
            e = object.__getattribute__(self, "_e")
            st = object.__getattribute__(self, "_st")
            a = getattr(e, n)
            if n == "engine_nop":
                def enop(*args, **kw):
                    e.drain()
                    st["r"].clear(); st["w"].clear()
                    return a(*args, **kw)
                return enop
            if n in _DRAIN_OPS:
                def call(*args, **kw):
                    wr, rd = [], []
                    if n == "tensor_copy" and args:
                        wr.append(_ap_rng(args[0]))
                        rd.extend(_ap_rng(x) for x in args[1:])
                    else:
                        rd.extend(_ap_rng(x) for x in args)
                    for k, v_ in kw.items():
                        r = _ap_rng(v_)
                        if r is None:
                            continue
                        (wr if k in ("out", "accum_out") else rd).append(r)
                    wr = [x for x in wr if x is not None]
                    rd = [x for x in rd if x is not None]
                    conflict = (len(st["w"]) + len(st["r"]) > 48)
                    if not conflict:
                        for x in wr:
                            if any(_overlap(x, y) for y in st["w"]) or                                any(_overlap(x, y) for y in st["r"]):
                                conflict = True
                                break
                        if not conflict:
                            for x in rd:
                                if any(_overlap(x, y) for y in st["w"]):
                                    conflict = True
                                    break
                    if conflict:
                        e.drain()
                        st["r"].clear(); st["w"].clear()
                    st["w"].extend(wr)
                    st["r"].extend(rd)
                    return a(*args, **kw)
                return call
            return a

    def bvec(f):
        st = _dw_state.setdefault("v", {"r": [], "w": []})
        block.vector(lambda v: f(_DW(v, st)))

    def bsca(f):
        st = _dw_state.setdefault("s", {"r": [], "w": []})
        block.scalar(lambda s: f(_DW(s, st)))

    def bgps(f):
        st = _dw_state.setdefault("g", {"r": [], "w": []})
        block.gpsimd(lambda gp: f(_DW(gp, st)))

    def raw_transpose(t, out_ap, in_ap, first):
        return t.matmul(out_ap, in_ap, IDT[0:in_ap.shape[0], 0:in_ap.shape[0]],
                        start=first, stop=True, is_transpose=True, skip_group_check=True)

    def sel_mm(t, out_ap, row, n, first):
        return t.matmul(out_ap, SELE[0:8, row * 128:row * 128 + out_ap.shape[0]],
                        STKT[0:8, 0:n], start=first, stop=True, skip_group_check=True)

    def barrier():
        def eV(v): inc(v.engine_nop(), "eV")
        def eG(gp): inc(gp.engine_nop(), "eG")
        bvec(eV); bgps(eG)
        tgt = dict(cnt)
        def wT(t): [wg(t, k, tgt[k]) for k in ("eV", "eS", "eG")]
        def wV(v): [wg(v, k, tgt[k]) for k in ("eT", "eS", "eG")]
        def wS(s): [wg(s, k, tgt[k]) for k in ("eT", "eV", "eG")]
        def wG(gp): [wg(gp, k, tgt[k]) for k in ("eT", "eV", "eS")]
        def wQ(q): [wg(q, k, tgt[k]) for k in ("eT", "eV", "eS", "eG")]
        block.tensor(wT); bvec(wV); bsca(wS); block.gpsimd(wG)
        block.sync(wQ)

    last_stats = {"eS": 0, "eV": 0}

    def ln_stats(zj, W, save=None, use_bc=False, wV=None):
        ng = (W + 127) // 128
        gw = [min(128, W - 128 * gi) for gi in range(ng)]
        eS_base = cnt["eS"]
        eS_prior = last_stats["eS"]
        eV_prior = last_stats["eV"]
        for gi in range(ng):
            def eT_g(t, gi=gi):
                if use_bc:
                    wg(t, "eV")
                    if gi == 0:
                        wg(t, "eS", eS_prior)
                    else:
                        wg(t, "eS", eS_base + gi)
                elif gi < 2:
                    wg(t, "eS", eS_prior)
                    wg(t, "eV", eV_prior)
                    if wV is not None:
                        wg(t, "eV", wV)
                else:
                    wg(t, "eS", eS_base + (gi - 1))
                slot = 0 if use_bc else (gi % 2) * 1024
                pst = PS_BC if use_bc else PS_T
                for j in range(J):
                    co = slot + j * 128
                    ins = raw_transpose(t, pst[0:gw[gi], co:co + 128],
                                        zj(j)[:, 128 * gi:128 * gi + gw[gi]], (co % 512) == 0)
                inc(ins, "eT")
            block.tensor(eT_g)
            eT_now = cnt["eT"]
            def eS_g(s, gi=gi, eT_now=eT_now):
                wg(s, "eT", eT_now)
                slot = 0 if use_bc else (gi % 2) * 1024
                pst = PS_BC if use_bc else PS_T
                s.activation(out=DUMP[0:gw[gi], 0:1024], in_=pst[0:gw[gi], slot:slot + 1024],
                             func=AF.Copy, accum_out=ACC1[0:gw[gi], gi:gi + 1])
                ins = s.activation(out=DUMP[0:gw[gi], 0:1024], in_=pst[0:gw[gi], slot:slot + 1024],
                                   func=AF.Square, accum_out=ACC2[0:gw[gi], gi:gi + 1])
                inc(ins, "eS")
            bsca(eS_g)
        last_stats["eS"] = cnt["eS"]

        P = max(gw)
        def emit_V1(v):
            wg(v, "eS")
            v.tensor_scalar(out=STAT[0:P, 0:ng], in0=ACC1[0:P, 0:ng],
                            scalar1=1.0 / D, scalar2=None, op0=ALU.mult)
            v.tensor_scalar(out=STAT[0:P, 4:4 + ng], in0=ACC2[0:P, 0:ng],
                            scalar1=1.0 / D, scalar2=EPS, op0=ALU.mult, op1=ALU.add)
            v.tensor_tensor(out=STAT[0:P, 8:8 + ng], in0=STAT[0:P, 0:ng],
                            in1=STAT[0:P, 0:ng], op=ALU.mult)
            ins = v.tensor_tensor(out=STAT[0:P, 8:8 + ng], in0=STAT[0:P, 4:4 + ng],
                                  in1=STAT[0:P, 8:8 + ng], op=ALU.subtract)
            inc(ins, "eV")
        bvec(emit_V1)

        def emit_S2(s):
            wg(s, "eV")
            s.activation(out=STAT[0:P, 12:12 + ng], in_=STAT[0:P, 8:8 + ng], func=AF.Ln)
            ins = s.activation(out=STK[0:P, 0:ng], in_=STAT[0:P, 12:12 + ng],
                               func=AF.Exp, scale=CST[0:P, C_NHALF:C_NHALF + 1])
            inc(ins, "eS")
        bsca(emit_S2)

        def emit_V2(v):
            wg(v, "eS")
            ins = v.tensor_tensor(out=STK[0:P, 4:4 + ng], in0=STAT[0:P, 0:ng],
                                  in1=STK[0:P, 0:ng], op=ALU.mult)
            inc(ins, "eV")
            if save is not None:
                sr, sm, scol = save
                v.tensor_copy(sr[0:P, scol:scol + ng], STK[0:P, 0:ng])
                ins = v.tensor_copy(sm[0:P, scol:scol + ng], STK[0:P, 4:4 + ng])
                inc(ins, "eV")
        bvec(emit_V2)
        ln_bcast(P, ng, gw, use_bc)

    def ln_bcast(P, ng, gw, use_bc=False):
        pst = PS_BC if use_bc else PS_T
        def emit_T2(t):
            wg(t, "eV")
            if use_bc:
                wg(t, "eS")
            ins = raw_transpose(t, pst[0:8, 0:P], STK[0:P, 0:8], True)
            inc(ins, "eT")
        block.tensor(emit_T2)

        def emit_V3(v):
            wg(v, "eT")
            ins = v.tensor_copy(STKT[0:8, 0:P], pst[0:8, 0:P])
            inc(ins, "eV")
        bvec(emit_V3)
        last_stats["eV"] = cnt["eV"]

        def emit_T3(t):
            wg(t, "eV")
            for gi in range(ng):
                co = gi * 128
                ins = sel_mm(t, PS_BC[:, co:co + gw[gi]], gi, gw[gi], (co % 512) == 0)
            for gi in range(ng):
                co = 512 + gi * 128
                ins = sel_mm(t, PS_BC[:, co:co + gw[gi]], 4 + gi, gw[gi], (co % 512) == 0)
            inc(ins, "eT")
        block.tensor(emit_T3)

    def p0_sync(q):
        for nm in ("cw0", "cw1", "cw2", "cb", "la", "sb", "sc", "sd", "ob", "lnw", "lnb", "b2"):
            dinc(q.dma_start(out=g[nm][:], in_=vec_in[nm]), "qx")
        dinc(q.dma_start(out=IPB[:], in_=ipb_d), "qx")
        dinc(q.dma_start(out=B1[:], in_=b1_d), "qx")
        dinc(q.dma_start(out=CST[:], in_=cst_d), "qx")
        dinc(q.dma_start(out=SELE[:], in_=sele_d), "qx")
        dinc(q.dma_start(out=IDT[:], in_=idt_d), "qx")
        dinc(q.dma_start(out=SEL[:], in_=sel_d), "qx")
        dinc(q.dma_start(out=MSK[:], in_=msk_d), "qx")
        dinc(q.dma_start(out=XT[:, 0:J].rearrange("p (j t) -> p j t", j=J),
                         in_=xs[:, 0:1].rearrange("(j p) t -> p j t", p=128),
                         allow_slow_non_contiguous=True), "qx")
        dinc(q.dma_start(out=XT[:, J:2 * J].rearrange("p (j t) -> p j t", j=J),
                         in_=xs[:, TC + 1:TC + 2].rearrange("(j p) t -> p j t", p=128),
                         allow_slow_non_contiguous=True), "qx")
    block.sync(p0_sync)
    q0 = cnt["qx"]

    def p0_s(s):
        wg(s, "qx", q0)
        ins = s.activation(out=AV[:], in_=g["la"][:], func=AF.Sigmoid)
        inc(ins, "eS")
    bsca(p0_s)

    def p0_v(v):
        wg(v, "qx", q0)
        fill(v, SST[:], 0.0)
        fill(v, RR_[:], 1.0)
        fill(v, STK[:], 0.0)
        fill(v, STAT[:], 0.0)
        fill(v, SCR[:, L:2 * L], 0.0)
        wg(v, "eS")
        for j in range(J):
            v.tensor_tensor_scan(out=SCR[:, 0:L],
                                 data0=AV[:, j:j + 1].to_broadcast([128, L]),
                                 data1=SCR[:, L:2 * L], initial=1.0,
                                 op0=ALU.mult, op1=ALU.add)
            v.tensor_copy(A512[:, j:j + 1], SCR[:, L - 1:L])
        v.tensor_tensor(out=A2K[:], in0=A512[:], in1=A512[:], op=ALU.mult)
        ins = v.tensor_tensor(out=A2K[:], in0=A2K[:], in1=A2K[:], op=ALU.mult)
        inc(ins, "eV")
    bvec(p0_v)

    ES2 = ExitStack()
    WVH = ES2.enter_context(nc.sbuf_tensor("wvh_t", [128, J * D], F16))
    WVL = ES2.enter_context(nc.sbuf_tensor("wvl_t", [128, J * D], F16))
    VW = [ES2.enter_context(nc.sbuf_tensor("vw%d_t" % p, [128, J * 514], F32))
          for p in range(2)]

    def wsplit(src_dram, n_chunks, wdt, H, Lo):
        ev0 = cnt["eV"]
        qv = []
        def w_dma(q):
            for k in range(n_chunks):
                wg(q, "eV", ev0 if k < 2 else ev0 + (k - 1))
                key = "qwA" if k % 2 == 0 else "qwB"
                dinc(q.dma_start(out=SCA[:, (k % 2) * wdt:(k % 2) * wdt + wdt],
                                 in_=src_dram(k)), key)
                qv.append((key, cnt[key]))
        block.sync(w_dma)
        def w_split(v):
            for k in range(n_chunks):
                wg(v, qv[k][0], qv[k][1])
                src = SCA[:, (k % 2) * wdt:(k % 2) * wdt + wdt]
                v.tensor_copy(H[:, k * wdt:(k + 1) * wdt], src)
                ins = v.scalar_tensor_tensor(out=Lo[:, k * wdt:(k + 1) * wdt],
                                             in0=H[:, k * wdt:(k + 1) * wdt], scalar=-1.0,
                                             in1=src, op0=ALU.mult, op1=ALU.add)
                inc(ins, "eV")
        bvec(w_split)

    wsplit(lambda k: wi[128 * k:128 * (k + 1), D:2 * D], J, D, WVH, WVL)
    barrier()

    def xh_col(tt, j):
        return XT[:, tt * J + j:tt * J + j + 1]

    def p1_T(t):
        for j in range(J):
            co = j * 128
            ins = raw_transpose(t, PS_T[0:2, co:co + 128], XT[:, j:2 * J:J], (co % 512) == 0)
        inc(ins, "eT")
    block.tensor(p1_T)

    def p1_S(s):
        wg(s, "eT")
        s.activation(out=DUMP[0:2, 0:1024], in_=PS_T[0:2, 0:1024], func=AF.Copy,
                     accum_out=ACC1[0:2, 0:1])
        ins = s.activation(out=DUMP[0:2, 0:1024], in_=PS_T[0:2, 0:1024], func=AF.Square,
                           accum_out=ACC2[0:2, 0:1])
        inc(ins, "eS")
    bsca(p1_S)

    def p1_V(v):
        wg(v, "eS")
        v.tensor_scalar(out=STAT[0:2, 0:1], in0=ACC1[0:2, 0:1], scalar1=1.0 / D,
                        scalar2=None, op0=ALU.mult)
        v.tensor_scalar(out=STAT[0:2, 4:5], in0=ACC2[0:2, 0:1], scalar1=1.0 / D,
                        scalar2=EPS, op0=ALU.mult, op1=ALU.add)
        v.tensor_tensor(out=STAT[0:2, 8:9], in0=STAT[0:2, 0:1], in1=STAT[0:2, 0:1],
                        op=ALU.mult)
        ins = v.tensor_tensor(out=STAT[0:2, 8:9], in0=STAT[0:2, 4:5], in1=STAT[0:2, 8:9],
                              op=ALU.subtract)
        inc(ins, "eV")
    bvec(p1_V)

    def p1_S2(s):
        wg(s, "eV")
        s.activation(out=STAT[0:2, 12:13], in_=STAT[0:2, 8:9], func=AF.Ln)
        ins = s.activation(out=STK[0:2, 0:1], in_=STAT[0:2, 12:13], func=AF.Exp,
                           scale=CST[0:2, C_NHALF:C_NHALF + 1])
        inc(ins, "eS")
    bsca(p1_S2)

    def p1_V2(v):
        wg(v, "eS")
        ins = v.tensor_tensor(out=STK[0:2, 4:5], in0=STAT[0:2, 0:1], in1=STK[0:2, 0:1],
                              op=ALU.mult)
        inc(ins, "eV")
    bvec(p1_V2)
    ln_bcast(2, 1, [2])

    def p1_V3(v):
        wg(v, "eT")
        for j in range(J):
            for tt in range(2):
                c0 = tt * J + j
                v.tensor_tensor(out=SCB[:, c0:c0 + 1], in0=xh_col(tt, j),
                                in1=PS_BC[:, tt:tt + 1], op=ALU.mult)
                v.tensor_tensor(out=SCB[:, c0:c0 + 1], in0=SCB[:, c0:c0 + 1],
                                in1=PS_BC[:, 512 + tt:513 + tt], op=ALU.subtract)
        v.tensor_copy(XHH[:, 0:2 * J], SCB[:, 0:2 * J])
        ins = v.scalar_tensor_tensor(out=XHL[:, 0:2 * J], in0=XHH[:, 0:2 * J], scalar=-1.0,
                                     in1=SCB[:, 0:2 * J], op0=ALU.mult, op1=ALU.add)
        inc(ins, "eV")
    bvec(p1_V3)

    def p1_T2(t):
        wg(t, "eV")
        for o in range(J):
            ps = PS_MM[:, o * 2:o * 2 + 2]
            nmm = 0
            for k in range(J):
                for (wt, xt_) in ((WVH, XHH), (WVL, XHH), (WVH, XHL)):
                    ins = t.matmul(ps, wt[:, k * D + o * 128:k * D + (o + 1) * 128],
                                   xt_[:, k:2 * J:J], start=(nmm == 0), stop=(nmm == 23),
                                   skip_group_check=True)
                    nmm += 1
        inc(ins, "eT")
    block.tensor(p1_T2)

    def p1_S3(s):
        wg(s, "eT")
        for j in range(J):
            ps = PS_MM[:, j * 2:j * 2 + 2]
            ins = s.activation(out=VH[:, 2 * j:2 * j + 2], in_=ps, func=AF.Silu,
                               bias=IPB[:, 8 + j:9 + j], scale=1.0)
            inc(ins, "eS")
    bsca(p1_S3)

    def p1_V4(v):
        wg(v, "eS")
        for j in range(J):
            ins = v.tensor_tensor(out=VH[:, 2 * j:2 * j + 2], in0=VH[:, 2 * j:2 * j + 2],
                                  in1=MSK[:, 0:2], op=ALU.mult)
        inc(ins, "eV")
    bvec(p1_V4)
    barrier()

    def abs_chunk(a):
        if a < 128:
            return w1n[128 * (a // 4):128 * (a // 4 + 1),
                       256 * (a % 4):256 * (a % 4) + 256]
        b = a - 128
        return w2t[128 * (b // 4):128 * (b // 4 + 1), 256 * (b % 4):256 * (b % 4) + 256]

    qab_v = {}
    abs_eS = {}
    ABSQ = ["qaA", "qaB"]
    def emit_abs_batch(a0, n):
        def s_abs(s):
            for a in range(a0, a0 + n):
                key = ABSQ[a % 2]
                if a - 2 >= 0:
                    wg(s, "eS", abs_eS[a - 2])
                dinc(s.dma_start(out=TW4[:, (a % 2) * 256:(a % 2) * 256 + 256],
                                 in_=abs_chunk(a)), key)
                qab_v[a] = (key, cnt[key])
                if a > a0:
                    p = a - 1
                    wg(s, qab_v[p][0], qab_v[p][1])
                    ins = s.activation(out=DUMP[:, (p % 4) * 256:(p % 4) * 256 + 256],
                                       in_=TW4[:, (p % 2) * 256:(p % 2) * 256 + 256],
                                       func=AF.Abs, accum_out=AS1[:, p:p + 1])
                    inc(ins, "eS")
                    abs_eS[p] = cnt["eS"]
            p = a0 + n - 1
            wg(s, qab_v[p][0], qab_v[p][1])
            ins = s.activation(out=DUMP[:, (p % 4) * 256:(p % 4) * 256 + 256],
                               in_=TW4[:, (p % 2) * 256:(p % 2) * 256 + 256],
                               func=AF.Abs, accum_out=AS1[:, p:p + 1])
            inc(ins, "eS")
            abs_eS[p] = cnt["eS"]
        bsca(s_abs)

    def load_x(q, i):
        dinc(q.dma_start(out=XT[:].rearrange("p (j t) -> p j t", j=J),
                         in_=xs[:, 1 + i * L:1 + (i + 1) * L].rearrange("(j p) t -> p j t", p=128)),
             "qx")

    def in_proj_v_phase(i):
        eV_prev = cnt["eV"]
        def p2_lx(q, i=i):
            wg(q, "eG")
            load_x(q, i)
        block.sync(p2_lx)
        xq = cnt["qx"]

        def zj(j):
            return XT[:, j * L:(j + 1) * L]
        def wait_x(t):
            wg(t, "qx", xq)
        block.tensor(wait_x)
        ln_stats(zj, L, save=(ST1R, ST1M, 4 * i))

        eV_ap0 = cnt["eV"]
        def emit_apply(v):
            wg(v, "eT")
            wg(v, "qx", xq)
            for j in range(J):
                v.tensor_tensor(out=zj(j), in0=zj(j), in1=PS_BC[:, 0:L], op=ALU.mult)
                ins = v.tensor_tensor(out=zj(j), in0=zj(j), in1=PS_BC[:, 512:512 + L],
                                      op=ALU.subtract)
                inc(ins, "eV")
        bvec(emit_apply)

        eS_hi0 = cnt["eS"]
        def emit_hi(s):
            for j in range(J):
                wg(s, "eV", eV_ap0 + j + 1)
                ins = s.activation(out=XPH[:, j * L:(j + 1) * L], in_=zj(j), func=AF.Copy)
                inc(ins, "eS")
        bsca(emit_hi)

        eT_pmm = cnt["eT"]
        def emit_split(gp):
            wg(gp, "eT", eT_pmm)
            for j in range(J):
                wg(gp, "eS", eS_hi0 + j + 1)
                ins = gp.tensor_tensor(out=XPL[:, j * L:(j + 1) * L], in0=zj(j),
                                       in1=XPH[:, j * L:(j + 1) * L], op=ALU.subtract)
                inc(ins, "eG")
        bgps(emit_split)

        if debug and i == 0:
            def dbg_xn(v):
                for j in range(J):
                    v.tensor_tensor(out=SCA[:, j * L:(j + 1) * L], in0=zj(j),
                                    in1=PS_BC[:, 0:L], op=ALU.mult)
                    ins = v.tensor_tensor(out=SCA[:, j * L:(j + 1) * L],
                                          in0=SCA[:, j * L:(j + 1) * L],
                                          in1=PS_BC[:, 512:512 + L], op=ALU.subtract)
                inc(ins, "eV")
            bvec(dbg_xn)
            def dbg_xn_dma(q):
                wg(q, "eV")
                dinc(q.dma_start(out=dbg["xn0"].rearrange("(j p) t -> p j t", p=128),
                                 in_=SCA[:].rearrange("p (j t) -> p j t", j=J)), "qs")
            block.sync(dbg_xn_dma)

        eG_split = cnt["eG"]
        eT0 = cnt["eT"]
        eS0 = cnt["eS"]
        def emit_mm(t):
            wg(t, "eG", eG_split)
            wg(t, "eS", eS0)
            for o in range(J):
                if o >= 2:
                    wg(t, "eS", eS0 + (o - 1))
                half = (o % 2) * 512
                ps = PS_MM[:, half:half + L]
                nmm = 0
                for k in range(J):
                    for (wt, xt_) in ((WVH, XPH), (WVL, XPH), (WVH, XPL)):
                        ins = t.matmul(ps, wt[:, k * D + o * 128:k * D + (o + 1) * 128],
                                       xt_[:, k * L:(k + 1) * L],
                                       start=(nmm == 0), stop=(nmm == 23))
                        nmm += 1
                inc(ins, "eT")
        block.tensor(emit_mm)

        def emit_evac(s, i=i):
            wg(s, "eV", eV_prev)
            for o in range(J):
                wg(s, "eT", eT0 + o + 1)
                half = (o % 2) * 512
                ps = PS_MM[:, half:half + L]
                ins = s.activation(out=VW[i % 2][:, o * 514 + 1:o * 514 + 513], in_=ps,
                                   func=AF.Silu, bias=IPB[:, 8 + o:9 + o], scale=1.0)
                inc(ins, "eS")
        bsca(emit_evac)

    def mixer_phase(k):
        vwk = VW[k % 2]
        eG_mg0 = cnt["eG"]
        def emit_mix_g(gp):
            wg(gp, "eS")
            if k == 0:
                gp.tensor_copy(vwk[:, 0:J * 514:514], VH[:, 0:2 * J:2])
            else:
                gp.tensor_copy(vwk[:, 0:J * 514:514], VW[(k - 1) % 2][:, 512:J * 514:514])
            if k == NST - 1:
                ins = gp.tensor_copy(vwk[:, 513:J * 514:514], VH[:, 1:2 * J:2])
            else:
                ins = gp.tensor_copy(vwk[:, 513:J * 514:514], VW[(k + 1) % 2][:, 1:J * 514:514])
            inc(ins, "eG")
        bgps(emit_mix_g)

        eG_fix = cnt["eG"]
        def emit_mix_v(v):
            wg(v, "eG", eG_fix)
            wg(v, "eS")
            def vv(a, b, j):
                return vwk[:, j * 514 + a:j * 514 + b]
            for jp in range(0, J, 2):
                for j in (jp, jp + 1):
                    cbuf = SCR[:, (j % 2) * L:(j % 2) * L + L]
                    v.tensor_scalar(out=cbuf, in0=vv(0, 512, j), scalar1=g["cw0"][:, j:j + 1],
                                    scalar2=g["cb"][:, j:j + 1], op0=ALU.mult, op1=ALU.add)
                for j in (jp, jp + 1):
                    cbuf = SCR[:, (j % 2) * L:(j % 2) * L + L]
                    v.scalar_tensor_tensor(out=cbuf, in0=vv(1, 513, j), scalar=g["cw1"][:, j:j + 1],
                                           in1=cbuf, op0=ALU.mult, op1=ALU.add)
                for j in (jp, jp + 1):
                    cbuf = SCR[:, (j % 2) * L:(j % 2) * L + L]
                    v.scalar_tensor_tensor(out=cbuf, in0=vv(2, 514, j), scalar=g["cw2"][:, j:j + 1],
                                           in1=cbuf, op0=ALU.mult, op1=ALU.add)
                for j in (jp, jp + 1):
                    cbuf = SCR[:, (j % 2) * L:(j % 2) * L + L]
                    v.tensor_scalar(out=SCA[:, j * L:(j + 1) * L], in0=cbuf,
                                    scalar1=g["sb"][:, j:j + 1], scalar2=None, op0=ALU.mult)
                for j in (jp, jp + 1):
                    v.tensor_tensor_scan(out=SCB[:, j * L:(j + 1) * L],
                                         data0=AV[:, j:j + 1].to_broadcast([128, L]),
                                         data1=SCA[:, j * L:(j + 1) * L],
                                         initial=SST[:, j:j + 1], op0=ALU.mult, op1=ALU.add)
                for j in (jp, jp + 1):
                    v.tensor_copy(SST[:, j:j + 1], SCB[:, j * L + L - 1:j * L + L])
                for j in (jp, jp + 1):
                    v.tensor_scalar(out=SCB[:, j * L:(j + 1) * L], in0=SCB[:, j * L:(j + 1) * L],
                                    scalar1=g["sc"][:, j:j + 1], scalar2=None, op0=ALU.mult)
                for j in (jp, jp + 1):
                    yj = Y[:, j * TC + k * L:j * TC + (k + 1) * L]
                    cbuf = SCR[:, (j % 2) * L:(j % 2) * L + L]
                    ins = v.scalar_tensor_tensor(out=yj, in0=cbuf, scalar=g["sd"][:, j:j + 1],
                                                 in1=SCB[:, j * L:(j + 1) * L],
                                                 op0=ALU.mult, op1=ALU.add)
                    inc(ins, "eV")
        bvec(emit_mix_v)

    for i in range(NST):
        in_proj_v_phase(i)
        if i >= 1:
            mixer_phase(i - 1)
    mixer_phase(NST - 1)
    barrier()

    if debug:
        def dbg_p2(q):
            dinc(q.dma_start(out=dbg["yloc"].rearrange("(j p) t -> p j t", p=128),
                             in_=Y[:].rearrange("p (j t) -> p j t", j=J)), "qs")
            dinc(q.dma_start(out=dbg["vw0"].rearrange("(j p) t -> p j t", p=128),
                             in_=VW[1][:].rearrange("p (j t) -> p j t", j=J)), "qs")
        block.sync(dbg_p2)
        barrier()
    ES2.close()

    def p3_g(gp):
        dinc(gp.dma_start(out=cc_in.ap(), in_=SST[:]), "qg")
        wg(gp, "qg")
        ins = gp.collective_compute("AllGather", ALU.bypass, ins=[cc_in.ap()],
                                    outs=[cc_out.ap()], replica_groups=[list(range(NCORE))])
        inc(ins, "cc")
        wg(gp, "cc")
        for c in range(NCORE):
            dinc(gp.dma_start(out=SALL[:, c * J:(c + 1) * J], in_=cc_out.ap()[c]), "qg")
    block.gpsimd(p3_g)

    def p3_v(v):
        wg(v, "qg")
        for j in range(J):
            v.tensor_tensor_scan(out=PIN[:, j * NCORE:(j + 1) * NCORE],
                                 data0=A2K[:, j:j + 1].to_broadcast([128, NCORE]),
                                 data1=SALL[:, j:NCORE * J:J], initial=0.0,
                                 op0=ALU.mult, op1=ALU.add)
            v.tensor_tensor(out=PIN[:, j * NCORE:(j + 1) * NCORE],
                            in0=PIN[:, j * NCORE:(j + 1) * NCORE], in1=SEL[:], op=ALU.mult)
            ins = v.tensor_reduce(out=HIN[:, j:j + 1], in_=PIN[:, j * NCORE:(j + 1) * NCORE],
                                  axis=AXX, op=ALU.add)
            inc(ins, "eV")
    bvec(p3_v)
    if debug:
        def dbg_p3(q):
            wg(q, "eV")
            dinc(q.dma_start(out=dbg["hin"], in_=HIN[:]), "qs")
        block.sync(dbg_p3)
    barrier()

    ES4 = ExitStack()
    TW4 = ES4.enter_context(nc.sbuf_tensor("tw4_t", [128, 2 * 256], F32))
    WUH = ES4.enter_context(nc.sbuf_tensor("wuh_t", [128, J * D], F16))
    WUL = ES4.enter_context(nc.sbuf_tensor("wul_t", [128, J * D], F16))
    WOH = ES4.enter_context(nc.sbuf_tensor("woh_t", [128, J * D], F16))
    WOL = ES4.enter_context(nc.sbuf_tensor("wol_t", [128, J * D], F16))

    wsplit(lambda k: wi[128 * k:128 * (k + 1), 0:D], J, D, WUH, WUL)
    wsplit(lambda k: wo[128 * k:128 * (k + 1), :], J, D, WOH, WOL)
    barrier()

    eV_xn_end = [0]
    q3_x = [0]
    for i in range(NST):
        eV_p4prev = cnt["eV"]
        eS_p4prev = cnt["eS"]
        eT_p4prev = cnt["eT"]
        def p4_lx(q, i=i, w=eV_xn_end[0]):
            wg(q, "eV", w)
            load_x(q, i)
        block.sync(p4_lx)
        xq = cnt["qx"]

        def p4_stk(v, i=i):
            v.tensor_copy(STK[:, 0:4], ST1R[:, 4 * i:4 * i + 4])
            ins = v.tensor_copy(STK[:, 4:8], ST1M[:, 4 * i:4 * i + 4])
            inc(ins, "eV")
        bvec(p4_stk)
        ln_bcast(128, 4, [128, 128, 128, 128])

        eV_xn0 = cnt["eV"]
        def p4_xn(v, i=i):
            wg(v, "eT")
            wg(v, "qx", xq)
            for j in range(J):
                a = SCA[:, j * L:(j + 1) * L]
                v.tensor_tensor(out=a, in0=XT[:, j * L:(j + 1) * L], in1=PS_BC[:, 0:L],
                                op=ALU.mult)
                ins = v.tensor_tensor(out=a, in0=a, in1=PS_BC[:, 512:512 + L],
                                      op=ALU.subtract)
                inc(ins, "eV")
        bvec(p4_xn)
        eV_xn_end[0] = cnt["eV"]

        eS_xh0 = cnt["eS"]
        def p4_xn_hi(s, i=i):
            wg(s, "eT", eT_p4prev)
            for j in range(J):
                wg(s, "eV", eV_xn0 + j + 1)
                ins = s.activation(out=XPH[:, j * L:(j + 1) * L],
                                   in_=SCA[:, j * L:(j + 1) * L], func=AF.Copy)
                inc(ins, "eS")
        bsca(p4_xn_hi)

        def p4_xn_split(gp, i=i):
            wg(gp, "eT", eT_p4prev)
            for j in range(J):
                wg(gp, "eS", eS_xh0 + j + 1)
                ins = gp.tensor_tensor(out=XPL[:, j * L:(j + 1) * L],
                                       in0=SCA[:, j * L:(j + 1) * L],
                                       in1=XPH[:, j * L:(j + 1) * L], op=ALU.subtract)
                inc(ins, "eG")
        bgps(p4_xn_split)

        eG_xn = cnt["eG"]
        eT_u0 = cnt["eT"]
        eS_u0 = cnt["eS"]
        def p4_umm(t, i=i):
            wg(t, "eG", eG_xn)
            wg(t, "eV", eV_p4prev)
            for o in range(J):
                if o >= 2:
                    wg(t, "eS", eS_u0 + (o - 1))
                half = (o % 2) * 512
                ps = PS_MM[:, half:half + L]
                nmm = 0
                for k in range(J):
                    for (wt, xt_) in ((WUH, XPH), (WUL, XPH), (WUH, XPL)):
                        ins = t.matmul(ps, wt[:, k * D + o * 128:k * D + (o + 1) * 128],
                                       xt_[:, k * L:(k + 1) * L],
                                       start=(nmm == 0), stop=(nmm == 23))
                        nmm += 1
                inc(ins, "eT")
        block.tensor(p4_umm)

        def p4_uevac(s, i=i):
            wg(s, "eV", eV_p4prev)
            for o in range(J):
                wg(s, "eT", eT_u0 + o + 1)
                half = (o % 2) * 512
                ps = PS_MM[:, half:half + L]
                ins = s.activation(out=SCB[:, o * L:(o + 1) * L], in_=ps, func=AF.Copy)
                inc(ins, "eS")
        bsca(p4_uevac)

        def p4_corr(v, i=i):
            v.tensor_tensor(out=COEF[:], in0=HIN[:], in1=RR_[:], op=ALU.mult)
            v.tensor_tensor(out=COEF[:], in0=COEF[:], in1=g["sc"][:], op=ALU.mult)
            fill(v, SCR[:, L:2 * L], 0.0)
            for j in range(J):
                v.tensor_tensor_scan(out=SCR[:, 0:L],
                                     data0=AV[:, j:j + 1].to_broadcast([128, L]),
                                     data1=SCR[:, L:2 * L], initial=1.0,
                                     op0=ALU.mult, op1=ALU.add)
                yj = Y[:, j * TC + i * L:j * TC + (i + 1) * L]
                ins = v.scalar_tensor_tensor(out=yj, in0=SCR[:, 0:L],
                                             scalar=COEF[:, j:j + 1], in1=yj,
                                             op0=ALU.mult, op1=ALU.add)
                inc(ins, "eV")
            ins = v.tensor_tensor(out=RR_[:], in0=RR_[:], in1=A512[:], op=ALU.mult)
            inc(ins, "eV")
        bvec(p4_corr)

        eV_corr = cnt["eV"]
        eS_sl0 = cnt["eS"]
        eV_mix_b = cnt["eV"]
        def p4_silu(s, i=i):
            wg(s, "eV", eV_corr)
            for j in range(J):
                if j >= 2:
                    wg(s, "eV", eV_mix_b + (j - 1))
                yj = Y[:, j * TC + i * L:j * TC + (i + 1) * L]
                ins = s.activation(out=SCR[:, (j % 2) * L:(j % 2) * L + L], in_=yj,
                                   func=AF.Silu)
                inc(ins, "eS")
        bsca(p4_silu)

        eT_umm = cnt["eT"]
        def p4_mix(v, i=i):
            wg(v, "eT", eT_umm)
            for j in range(J):
                wg(v, "eS", eS_sl0 + j + 1)
                m = SCB[:, j * L:(j + 1) * L]
                ins = v.scalar_tensor_tensor(out=m, in0=m, scalar=IPB[:, j:j + 1],
                                             in1=SCR[:, (j % 2) * L:(j % 2) * L + L],
                                             op0=ALU.add, op1=ALU.mult)
                inc(ins, "eV")
        bvec(p4_mix)

        eV_mx0 = cnt["eV"] - J
        eS_mh0 = cnt["eS"]
        def p4_mix_hi(s, i=i):
            for j in range(J):
                wg(s, "eV", eV_mx0 + j + 1)
                ins = s.activation(out=XPH[:, j * L:(j + 1) * L],
                                   in_=SCB[:, j * L:(j + 1) * L], func=AF.Copy)
                inc(ins, "eS")
        bsca(p4_mix_hi)

        def p4_mix_split(gp, i=i):
            for j in range(J):
                wg(gp, "eS", eS_mh0 + j + 1)
                m = SCB[:, j * L:(j + 1) * L]
                ins = gp.tensor_tensor(out=XPL[:, j * L:(j + 1) * L], in0=m,
                                       in1=XPH[:, j * L:(j + 1) * L], op=ALU.subtract)
                inc(ins, "eG")
        bgps(p4_mix_split)

        eG_mix = cnt["eG"]
        eT_o0 = cnt["eT"]
        eV_oe0 = cnt["eV"]
        def p4_omm(t, i=i):
            wg(t, "eG", eG_mix)
            for o in range(J):
                if o >= 2:
                    wg(t, "eV", eV_oe0 + (o - 1))
                half = (o % 2) * 512
                ps = PS_MM[:, half:half + L]
                nmm = 0
                for k in range(J):
                    for (wt, xt_) in ((WOH, XPH), (WOL, XPH), (WOH, XPL)):
                        ins = t.matmul(ps, wt[:, k * D + o * 128:k * D + (o + 1) * 128],
                                       xt_[:, k * L:(k + 1) * L],
                                       start=(nmm == 0), stop=(nmm == 23))
                        nmm += 1
                inc(ins, "eT")
        block.tensor(p4_omm)

        def p4_oevac(v, i=i):
            for o in range(J):
                wg(v, "eT", eT_o0 + o + 1)
                half = (o % 2) * 512
                ps = PS_MM[:, half:half + L]
                yj = Y[:, o * TC + i * L:o * TC + (i + 1) * L]
                ins = v.scalar_tensor_tensor(out=yj, in0=ps, scalar=g["ob"][:, o:o + 1],
                                             in1=SCA[:, o * L:(o + 1) * L],
                                             op0=ALU.add, op1=ALU.add)
                inc(ins, "eV")
        bvec(p4_oevac)

        eV_oev = cnt["eV"]
        def p4_x2load(q, i=i):
            wg(q, "eV", eV_oev)
            dinc(q.dma_start(out=SCA[:].rearrange("p (j t) -> p j t", j=J),
                             in_=xs[:, 1 + i * L:1 + (i + 1) * L].rearrange("(j p) t -> p j t", p=128)),
                 "q3A")
            q3_x[0] = cnt["q3A"]
        block.sync(p4_x2load)

        def zj2(j, i=i):
            return Y[:, j * TC + i * L:j * TC + (i + 1) * L]
        ln_stats(zj2, L, wV=cnt["eV"])
        def p4_ln2(v, i=i):
            wg(v, "eT")
            for j in range(J):
                a = SCB[:, j * L:(j + 1) * L]
                v.tensor_tensor(out=a, in0=zj2(j), in1=PS_BC[:, 0:L], op=ALU.mult)
                v.tensor_tensor(out=a, in0=a, in1=PS_BC[:, 512:512 + L], op=ALU.subtract)
                ins = v.tensor_scalar(out=a, in0=a, scalar1=g["lnw"][:, j:j + 1],
                                      scalar2=g["lnb"][:, j:j + 1], op0=ALU.mult,
                                      op1=ALU.add)
                inc(ins, "eV")
        bvec(p4_ln2)

        def zj3(j, i=i):
            return SCB[:, j * L:(j + 1) * L]
        ln_stats(zj3, L, wV=cnt["eV"])
        def p4_ln3(v, i=i, w=None):
            wg(v, "eT")
            wg(v, "q3A", q3_x[0])
            for j in range(J):
                a = SCB[:, j * L:(j + 1) * L]
                yj = Y[:, j * TC + i * L:j * TC + (i + 1) * L]
                v.tensor_tensor(out=a, in0=a, in1=PS_BC[:, 0:L], op=ALU.mult)
                v.tensor_tensor(out=a, in0=a, in1=PS_BC[:, 512:512 + L], op=ALU.subtract)
                ins = v.tensor_tensor(out=yj, in0=a, in1=SCA[:, j * L:(j + 1) * L],
                                      op=ALU.add)
                inc(ins, "eV")
        bvec(p4_ln3)
        emit_abs_batch(i * 64, 64)
    barrier()

    ES4.close()
    if debug:
        def dbg_p4(q):
            dinc(q.dma_start(out=dbg["x2"].rearrange("(j p) t -> p j t", p=128),
                             in_=Y[:].rearrange("p (j t) -> p j t", j=J)), "qs")
        block.sync(dbg_p4)
        barrier()

    EST = ExitStack()
    TW = EST.enter_context(nc.sbuf_tensor("tw_t", [128, 4 * 1024], F32))
    TSC32 = EST.enter_context(nc.sbuf_tensor("tsc32_t", [128, 2 * 1024], F32))
    TSB32 = EST.enter_context(nc.sbuf_tensor("tsb32_t", [128, 2 * 1024], F32))
    TSC16 = EST.enter_context(nc.sbuf_tensor("tsc16_t", [128, 2 * 1024], F16))
    TOUT = EST.enter_context(nc.sbuf_tensor("tout_t", [128, 2 * 1024], F16))
    W1LQ = ["qmA", "qmB"]
    W1SQ = ["qtA", "qtB"]
    W2SQ = ["qtC", "qtD"]

    def tern_scale_v(v):
        wg(v, "eS")
        v.tensor_reduce(out=STAT[:, 0:1], in_=AS1[:, 0:128], axis=AXX, op=ALU.add)
        v.tensor_reduce(out=STAT[:, 1:2], in_=AS1[:, 128:256], axis=AXX, op=ALU.add)
        ins = v.tensor_copy(AS1[:, 0:2], STAT[:, 0:2])
        inc(ins, "eV")
    bvec(tern_scale_v)

    def tern_scale_T(t):
        wg(t, "eV")
        ins = raw_transpose(t, PS_T[0:2, 0:128], AS1[:, 0:2], True)
        inc(ins, "eT")
    block.tensor(tern_scale_T)

    def tern_scale_v2(v):
        wg(v, "eT")
        v.tensor_reduce(out=STAT[0:2, 0:1], in_=PS_T[0:2, 0:128], axis=AXX, op=ALU.add)
        v.tensor_scalar(out=STAT[0:2, 1:2], in0=STAT[0:2, 0:1], scalar1=1.0 / (D * FF),
                        scalar2=1e-8, op0=ALU.mult, op1=ALU.max)
        v.tensor_scalar(out=STAT[0:2, 2:3], in0=STAT[0:2, 1:2], scalar1=0.5, scalar2=None,
                        op0=ALU.mult)
        ins = v.tensor_scalar(out=STAT[0:2, 3:4], in0=STAT[0:2, 2:3], scalar1=-1.0,
                              scalar2=None, op0=ALU.mult)
        inc(ins, "eV")
    bvec(tern_scale_v2)

    def tern_thr_T(t):
        wg(t, "eV")
        ins = raw_transpose(t, PS_T[0:2, 128:130], STAT[0:2, 2:4], False)
        inc(ins, "eT")
    block.tensor(tern_thr_T)

    def tern_thr_v(v):
        wg(v, "eT")
        ins = v.tensor_copy(STKT[0:2, 0:2], PS_T[0:2, 128:130])
        inc(ins, "eV")
    bvec(tern_thr_v)

    def tern_thr_T2(t):
        wg(t, "eV")
        sel_mm(t, PS_BC[:, 0:2], 0, 2, True)
        ins = sel_mm(t, PS_BC[:, 2:4], 1, 2, False)
        inc(ins, "eT")
    block.tensor(tern_thr_T2)

    def tern_thr_v2(v):
        wg(v, "eT")
        v.tensor_copy(THRW[:, 0:1], PS_BC[:, 0:1])
        v.tensor_copy(THRW[:, 1:2], PS_BC[:, 2:3])
        v.tensor_copy(THRW[:, 2:3], PS_BC[:, 1:2])
        ins = v.tensor_copy(THRW[:, 3:4], PS_BC[:, 3:4])
        inc(ins, "eV")
    bvec(tern_thr_v2)

    L5 = 512
    NS5 = TC // L5
    head_ev = {}

    def p5_head_emit(i):
        def zj4(j, i=i):
            return Y[:, j * TC + i * L5:j * TC + (i + 1) * L5]
        ln_stats(zj4, L5, use_bc=True)
        def p5_xn2(v, i=i):
            wg(v, "eT")
            wg(v, "qs")
            for j in range(J):
                a = SCA[:, j * L5:(j + 1) * L5]
                v.tensor_tensor(out=a, in0=zj4(j), in1=PS_BC[:, 0:L5], op=ALU.mult)
                ins = v.tensor_tensor(out=a, in0=a, in1=PS_BC[:, 512:512 + L5],
                                      op=ALU.subtract)
                inc(ins, "eV")
        bvec(p5_xn2)
        def p5_xn2_split(v, i=i):
            for j in range(J):
                a = SCA[:, j * L5:(j + 1) * L5]
                v.tensor_copy(XPH[:, j * L5:(j + 1) * L5], a)
                ins = v.scalar_tensor_tensor(out=XPL[:, j * L5:(j + 1) * L5],
                                             in0=XPH[:, j * L5:(j + 1) * L5], scalar=-1.0,
                                             in1=a, op0=ALU.mult, op1=ALU.add)
                inc(ins, "eV")
        bvec(p5_xn2_split)
        return cnt["eV"]

    head_ev[0] = p5_head_emit(0)

    qm_v = {}
    qg_v = {}
    qst_w1 = {}
    qst_w2 = {}
    t_eV = {}
    t_eT = {}
    t_eS = {}
    eS_t0 = cnt["eS"]
    for c in range(32):
        def w1_load(s, c=c):
            if c >= 2:
                wg(s, "eV", t_eV[("w1", c - 2)])
            else:
                wg(s, "eS", eS_t0)
            key = W1LQ[c % 2]
            dinc(s.dma_start(out=TW[:, (c % 2) * 1024:(c % 2) * 1024 + 1024],
                             in_=w1n[128 * c:128 * (c + 1), :]), key)
            qm_v[c] = (key, cnt[key])
        block.scalar(w1_load)

        def w1_cmp(v, c=c):
            wg(v, qm_v[c][0], qm_v[c][1])
            if c >= 2:
                wg(v, "eT", t_eT[c - 2])
            tw = TW[:, (c % 2) * 1024:(c % 2) * 1024 + 1024]
            tsa = TSC32[:, (c % 2) * 1024:(c % 2) * 1024 + 1024]
            tsb = TSB32[:, (c % 2) * 1024:(c % 2) * 1024 + 1024]
            v.tensor_scalar(out=tsa, in0=tw, scalar1=THRW[:, 0:1], scalar2=-1.0,
                            op0=ALU.is_gt, op1=ALU.add)
            ins = v.tensor_scalar(out=tsb, in0=tw, scalar1=THRW[:, 1:2], scalar2=None,
                                  op0=ALU.is_gt)
            inc(ins, "eV")
            t_eV[("w1", c)] = cnt["eV"]
        bvec(w1_cmp)

        def w1_tr(t, c=c):
            wg(t, "eV", t_eV[("w1", c)])
            if c >= 2:
                wg(t, "eS", t_eS[c - 2])
            ps = PS_MM[:, 0:1024] if c % 2 == 0 else PS_T[:, 0:1024]
            tsa = TSC32[:, (c % 2) * 1024:(c % 2) * 1024 + 1024]
            tsb = TSB32[:, (c % 2) * 1024:(c % 2) * 1024 + 1024]
            for k in range(8):
                raw_transpose(t, ps[:, 128 * k:128 * (k + 1)],
                              tsa[:, 128 * k:128 * (k + 1)], (k % 4 == 0))
                ins = raw_transpose(t, ps[:, 128 * k:128 * (k + 1)],
                                    tsb[:, 128 * k:128 * (k + 1)], False)
            inc(ins, "eT")
            t_eT[c] = cnt["eT"]
        block.tensor(w1_tr)

        def w1_evac(s, c=c):
            wg(s, "eT", t_eT[c])
            if c >= 2:
                wg(s, qst_w1[c - 2][0], qst_w1[c - 2][1])
            ps = PS_MM[:, 0:1024] if c % 2 == 0 else PS_T[:, 0:1024]
            ins = s.activation(out=TOUT[:, (c % 2) * 1024:(c % 2) * 1024 + 1024],
                               in_=ps, func=AF.Copy)
            inc(ins, "eS")
            t_eS[c] = cnt["eS"]
        bsca(w1_evac)

        def w1_store(q, c=c):
            wg(q, "eS", t_eS[c])
            key = W1SQ[c % 2]
            dinc(q.dma_start(out=w1b_dram.ap()[c],
                             in_=TOUT[:, (c % 2) * 1024:(c % 2) * 1024 + 1024]), key)
            qst_w1[c] = (key, cnt[key])
        block.sync(w1_store)

        def w2_load(gp, c=c):
            if c >= 2:
                wg(gp, "eV", t_eV[("w2", c - 2)])
            key = "qsA" if c % 2 == 0 else "qsB"
            dinc(gp.dma_start(out=TW[:, (2 + c % 2) * 1024:(2 + c % 2) * 1024 + 1024],
                              in_=w2t[128 * c:128 * (c + 1), :]), key)
            qg_v[c] = (key, cnt[key])
        block.gpsimd(w2_load)

        def w2_cmp(v, c=c):
            wg(v, qg_v[c][0], qg_v[c][1])
            if c >= 2:
                wg(v, qst_w2[c - 2][0], qst_w2[c - 2][1])
            tw = TW[:, (2 + c % 2) * 1024:(2 + c % 2) * 1024 + 1024]
            ts16 = TSC16[:, (c % 2) * 1024:(c % 2) * 1024 + 1024]
            v.tensor_scalar(out=ts16, in0=tw, scalar1=THRW[:, 2:3], scalar2=-1.0,
                            op0=ALU.is_gt, op1=ALU.add)
            ins = v.scalar_tensor_tensor(out=ts16, in0=tw, scalar=THRW[:, 3:4],
                                         in1=ts16, op0=ALU.is_gt, op1=ALU.add)
            inc(ins, "eV")
            t_eV[("w2", c)] = cnt["eV"]
        bvec(w2_cmp)

        def w2_store(q, c=c):
            wg(q, "eV", t_eV[("w2", c)])
            key = W2SQ[c % 2]
            dinc(q.dma_start(out=w2t_dram.ap()[128 * c:128 * (c + 1), :],
                             in_=TSC16[:, (c % 2) * 1024:(c % 2) * 1024 + 1024]), key)
            qst_w2[c] = (key, cnt[key])
        block.sync(w2_store)
    qfin = {k: cnt[k] for k in W1SQ + W2SQ}
    EST.close()

    ES5 = ExitStack()
    HGH = ES5.enter_context(nc.sbuf_tensor("hgh_t", [128, JF * L5], F16))
    HGL = ES5.enter_context(nc.sbuf_tensor("hgl_t", [128, JF * L5], F16))
    W2S = XT[:, 0:2048].bitcast(F16)
    W1SL = XT[:, 2048:4096].bitcast(F16)
    W1Q = ["qwA", "qwB", "q2A", "q2B"]
    W2Q = ["q3A", "q3B", "q3C", "q3D"]

    def p5_ob(o):
        if o < 4:
            return PS_T[:, o * 512:(o + 1) * 512]
        if o < 6:
            return PS_BC[:, (o - 4) * 512:(o - 3) * 512]
        return PS_MM[:, (o - 6) * 512:(o - 5) * 512]

    for i in range(NS5):
        if i > 0:
            head_ev[i] = p5_head_emit(i)
        eG_x2 = None
        eV_x2f = head_ev[i]
        eT_f0 = cnt["eT"]
        eS_g0 = cnt["eS"]
        eV_g0 = cnt["eV"]
        w1q = {}
        eT_pA = cnt["eT"]
        def p5_w1dma(q, i=i):
            for f in range(JF):
                if f >= 4:
                    wg(q, "eT", eT_f0 + (f - 4) + 1)
                elif i > 0:
                    wg(q, "eT", eT_pA)
                if i == 0:
                    wg(q, qst_w1[f][0], qst_w1[f][1])
                key = W1Q[f % 4]
                dinc(q.dma_start(out=W1SL[:, (f % 4) * 1024:(f % 4) * 1024 + 1024],
                                 in_=w1b_dram.ap()[f]), key)
                w1q[f] = (key, cnt[key])
        block.sync(p5_w1dma)
        eT_B0 = eT_f0 + JF
        w2q = {}
        def p5_w2dma(q, i=i):
            for f in range(JF):
                if f >= 4:
                    wg(q, "eT", eT_B0 + (f - 4) + 1)
                elif i > 0:
                    wg(q, "eT", eT_pA)
                if i == 0:
                    wg(q, qst_w2[f][0], qst_w2[f][1])
                key = W2Q[f % 4]
                dinc(q.dma_start(out=W2S[:, (f % 4) * 1024:(f % 4) * 1024 + 1024],
                                 in_=w2t_dram.ap()[128 * f:128 * (f + 1), :]), key)
                w2q[f] = (key, cnt[key])
        block.sync(p5_w2dma)

        def p5_ffnA(t, i=i):
            wg(t, "eV", eV_x2f)
            for f in range(JF):
                if f >= 2:
                    wg(t, "eS", eS_g0 + (f - 1))
                wg(t, w1q[f][0], w1q[f][1])
                half = (f % 2) * 512
                ps = PS_MM[:, half:half + L5]
                nmm = 0
                for k in range(J):
                    for xt_ in (XPH, XPL):
                        ins = t.matmul(ps, W1SL[:, (f % 4) * 1024 + k * 128:(f % 4) * 1024 + (k + 1) * 128],
                                       xt_[:, k * L5:(k + 1) * L5],
                                       start=(nmm == 0), stop=(nmm == 15))
                        nmm += 1
                inc(ins, "eT")
        block.tensor(p5_ffnA)

        eV_sp0 = cnt["eV"]
        def p5_gelu(s, i=i):
            for f in range(JF):
                wg(s, "eT", eT_f0 + f + 1)
                if f >= 2:
                    wg(s, "eV", eV_sp0 + (f - 1))
                half = (f % 2) * 512
                ps = PS_MM[:, half:half + L5]
                ins = s.activation(out=SCR[:, (f % 2) * L:(f % 2) * L + L5], in_=ps,
                                   func=AF.Gelu, bias=B1[:, f:f + 1], scale=1.0)
                inc(ins, "eS")
        bsca(p5_gelu)

        def p5_split(v, i=i):
            for f in range(JF):
                wg(v, "eS", eS_g0 + f + 1)
                if i == 0 and f == 0:
                    for kk in W1SQ + W2SQ:
                        wg(v, kk, qfin[kk])
                src = SCR[:, (f % 2) * L:(f % 2) * L + L5]
                v.tensor_copy(HGH[:, f * L5:(f + 1) * L5], src)
                ins = v.scalar_tensor_tensor(out=HGL[:, f * L5:(f + 1) * L5],
                                             in0=HGH[:, f * L5:(f + 1) * L5], scalar=-1.0,
                                             in1=src, op0=ALU.mult, op1=ALU.add)
                inc(ins, "eV")
        bvec(p5_split)

        def p5_ffnB(t, i=i):
            wg(t, "eV", eV_sp0 + JF)
            for f in range(JF):
                wg(t, w2q[f][0], w2q[f][1])
                for o in range(J):
                    for hi_ in (True, False):
                        ins = t.matmul(p5_ob(o),
                                       W2S[:, (f % 4) * 1024 + o * 128:(f % 4) * 1024 + (o + 1) * 128],
                                       HGH[:, f * L5:(f + 1) * L5] if hi_ else HGL[:, f * L5:(f + 1) * L5],
                                       start=(f == 0 and hi_), stop=(f == JF - 1 and not hi_))
                inc(ins, "eT")
        block.tensor(p5_ffnB)

        eT_all = cnt["eT"]
        def p5_evac(v, i=i):
            wg(v, "eT", eT_all)
            wg(v, "qs")
            for o in range(J):
                ins = v.tensor_scalar(out=SCB[:, o * L5:(o + 1) * L5],
                                      in0=p5_ob(o),
                                      scalar1=g["b2"][:, o:o + 1], scalar2=None,
                                      op0=ALU.add)
                inc(ins, "eV")
        bvec(p5_evac)

        def zj5(j, i=i):
            return SCB[:, j * L5:(j + 1) * L5]
        ln_stats(zj5, L5, use_bc=True)
        def p5_out(v, i=i):
            wg(v, "eT")
            for j in range(J):
                a = SCB[:, j * L5:(j + 1) * L5]
                yj = Y[:, j * TC + i * L5:j * TC + (i + 1) * L5]
                v.tensor_tensor(out=a, in0=a, in1=PS_BC[:, 0:L5], op=ALU.mult)
                v.tensor_tensor(out=a, in0=a, in1=PS_BC[:, 512:512 + L5], op=ALU.subtract)
                ins = v.tensor_tensor(out=a, in0=a, in1=yj, op=ALU.add)
                inc(ins, "eV")
        bvec(p5_out)

        def p5_store(q, i=i):
            wg(q, "eV")
            dinc(q.dma_start(out=out_d[:, i * L5:(i + 1) * L5].rearrange("(j p) t -> p j t", p=128),
                             in_=SCB[:, 0:J * L5].rearrange("p (j t) -> p j t", j=J)), "qs")
        block.sync(p5_store)

    barrier()
    ES5.close()
    qs_fin = cnt["qs"]
    def fin(gp):
        wg(gp, "qs", qs_fin)
    block.gpsimd(fin)

    ES.close()
    return nc


def _vec8(v):
    return np.ascontiguousarray(np.asarray(v, np.float32).reshape(J, 128).T)


def kernel(L0_attn_x, L0_attn_in_proj_w, L0_attn_in_proj_b, L0_attn_conv_w,
           L0_attn_conv_b, L0_attn_logit_a, L0_attn_ssm_b, L0_attn_ssm_c,
           L0_attn_ssm_d, L0_attn_out_proj_w, L0_attn_out_proj_b, L0_attn_ln_w,
           L0_attn_ln_b, L0_ffn_W1, L0_ffn_b1, L0_ffn_W2, L0_ffn_b2):
    x = np.asarray(L0_attn_x, dtype=np.float32)
    xT = np.ascontiguousarray(x[0].T)
    xTp = np.zeros((D, T_ALL + 2), np.float32)
    xTp[:, 1:T_ALL + 1] = xT

    wi_a = np.ascontiguousarray(np.asarray(L0_attn_in_proj_w, np.float32))
    wo_a = np.ascontiguousarray(np.asarray(L0_attn_out_proj_w, np.float32))
    w1n_a = np.ascontiguousarray(np.asarray(L0_ffn_W1, np.float32))
    w2t_a = np.ascontiguousarray(np.asarray(L0_ffn_W2, np.float32).T)

    cw = np.asarray(L0_attn_conv_w, np.float32)
    vecs = dict(
        cw0=_vec8(cw[:, 0, 0]), cw1=_vec8(cw[:, 0, 1]), cw2=_vec8(cw[:, 0, 2]),
        cb=_vec8(L0_attn_conv_b), la=_vec8(L0_attn_logit_a), sb=_vec8(L0_attn_ssm_b),
        sc=_vec8(L0_attn_ssm_c), sd=_vec8(L0_attn_ssm_d), ob=_vec8(L0_attn_out_proj_b),
        lnw=_vec8(L0_attn_ln_w), lnb=_vec8(L0_attn_ln_b), b2=_vec8(L0_ffn_b2),
    )
    ipb = np.ascontiguousarray(np.asarray(L0_attn_in_proj_b, np.float32).reshape(16, 128).T)
    b1 = np.ascontiguousarray(np.asarray(L0_ffn_b1, np.float32).reshape(JF, 128).T)

    cst = np.zeros((128, 8), np.float32)
    cst[:, C_NHALF] = -0.5
    sele = np.zeros((8, 1024), np.float32)
    for r in range(8):
        sele[r, r * 128:(r + 1) * 128] = 1.0
    idt = np.eye(128, dtype=np.float32)

    nc = build(debug=False)
    in_maps = []
    for c in range(NCORE):
        sel = np.zeros((128, J), np.float32)
        if c > 0:
            sel[:, c - 1] = 1.0
        msk = np.ones((128, 2), np.float32)
        if c == 0:
            msk[:, 0] = 0.0
        if c == NCORE - 1:
            msk[:, 1] = 0.0
        im = dict(xs=np.ascontiguousarray(xTp[:, c * TC:c * TC + TC + 2]),
                  wi=wi_a, wo=wo_a, w1n=w1n_a, w2t=w2t_a, ipb=ipb, b1=b1, cst=cst,
                  sele=sele, idt=idt, sel=sel, msk=msk, **vecs)
        in_maps.append(im)

    res = run_bass_kernel_spmd(nc, in_maps, core_ids=list(range(NCORE)))
    outs = [res.results[c]["out"] for c in range(NCORE)]
    full = np.concatenate(outs, axis=1)
    return np.ascontiguousarray(full.T)[None].astype(np.float32)


# version 13 (final)
# speedup vs baseline: 1.0243x; 1.0243x over previous
import numpy as np
from contextlib import ExitStack
import concourse.bass as bass
import concourse.mybir as mybir
from concourse.bass_utils import run_bass_kernel_spmd

F32 = mybir.dt.float32
F16 = mybir.dt.float16
AF = mybir.ActivationFunctionType
ALU = mybir.AluOpType
AXX = mybir.AxisListType.X

NCORE = 8
T_ALL = 16384
D = 1024
FF = 4096
J = 8
JF = 32
TC = T_ALL // NCORE
L = 512
NST = TC // L
LF = 256
NSF = TC // LF
EPS = 1e-5
C_NHALF = 1


def build(debug=False):
    nc = bass.Bass("TRN2", target_bir_lowering=False)

    xs = nc.dram_tensor("xs", [D, TC + 2], F32, kind="ExternalInput").ap()
    wi = nc.dram_tensor("wi", [D, 2 * D], F32, kind="ExternalInput").ap()
    wo = nc.dram_tensor("wo", [D, D], F32, kind="ExternalInput").ap()
    w1n = nc.dram_tensor("w1n", [FF, D], F32, kind="ExternalInput").ap()
    w2t = nc.dram_tensor("w2t", [FF, D], F32, kind="ExternalInput").ap()
    vec_in = {}
    for nm in ("cw0", "cw1", "cw2", "cb", "la", "sb", "sc", "sd", "ob", "lnw", "lnb", "b2"):
        vec_in[nm] = nc.dram_tensor(nm, [128, J], F32, kind="ExternalInput").ap()
    ipb_d = nc.dram_tensor("ipb", [128, 16], F32, kind="ExternalInput").ap()
    b1_d = nc.dram_tensor("b1", [128, JF], F32, kind="ExternalInput").ap()
    cst_d = nc.dram_tensor("cst", [128, 8], F32, kind="ExternalInput").ap()
    sele_d = nc.dram_tensor("sele", [8, 1024], F32, kind="ExternalInput").ap()
    idt_d = nc.dram_tensor("idt", [128, 128], F32, kind="ExternalInput").ap()
    sel_d = nc.dram_tensor("sel", [128, J], F32, kind="ExternalInput").ap()
    msk_d = nc.dram_tensor("msk", [128, 2], F32, kind="ExternalInput").ap()

    out_d = nc.dram_tensor("out", [D, TC], F32, kind="ExternalOutput").ap()
    dbg = {}
    if debug:
        for nm, shp in (("yloc", [D, TC]), ("x2", [D, TC]), ("xn0", [D, L]),
                        ("vw0", [D, 514]), ("hin", [128, J])):
            dbg[nm] = nc.dram_tensor("dbg_" + nm, shp, F32, kind="ExternalOutput").ap()

    w1b_dram = nc.dram_tensor("w1b_s", [JF, 128, D], F16)
    w2t_dram = nc.dram_tensor("w2t_s", [FF, D], F16)
    cc_in = nc.dram_tensor("cc_in", [128, J], F32)
    cc_out = nc.dram_tensor("cc_out", [NCORE, 128, J], F32)

    ES = ExitStack()
    sem_names = ["qx", "qs", "qg", "cc", "eT", "eV", "eS", "eG",
                 "qwA", "qwB", "q2A", "q2B", "qaA", "qaB", "qmA", "qmB",
                 "qsA", "qsB", "q3A", "q3B", "q3C", "q3D",
                 "qtA", "qtB", "qtC", "qtD"]
    sems = {n: ES.enter_context(nc.semaphore(n)) for n in sem_names}
    cnt = {n: 0 for n in sem_names}

    def inc(ins, key, n=1):
        cnt[key] += n
        ins.then_inc(sems[key], n)
        return ins

    def dinc(ins, key):
        return inc(ins, key, 16)

    def wg(eng, key, val=None):
        v = cnt[key] if val is None else val
        if v > 0:
            eng.wait_ge(sems[key], v)

    def fill(v, out_ap, val):
        P, W = out_ap.shape[0], out_ap.shape[1]
        return v.tensor_scalar(out=out_ap, in0=CST[0:P, 0:1].to_broadcast([P, W]),
                               scalar1=0.0, scalar2=float(val), op0=ALU.mult, op1=ALU.add)

    g = {}
    for nm in ("cw0", "cw1", "cw2", "cb", "la", "sb", "sc", "sd", "ob", "lnw", "lnb", "b2"):
        g[nm] = ES.enter_context(nc.sbuf_tensor(nm + "_t", [128, J], F32))
    IPB = ES.enter_context(nc.sbuf_tensor("ipb_t", [128, 16], F32))
    B1 = ES.enter_context(nc.sbuf_tensor("b1_t", [128, JF], F32))
    CST = ES.enter_context(nc.sbuf_tensor("cst_t", [128, 8], F32))
    SELE = ES.enter_context(nc.sbuf_tensor("sele_t", [8, 1024], F32))
    IDT = ES.enter_context(nc.sbuf_tensor("idt_t", [128, 128], F32))
    SEL = ES.enter_context(nc.sbuf_tensor("sel_t", [128, J], F32))
    MSK = ES.enter_context(nc.sbuf_tensor("msk_t", [128, 2], F32))
    AV = ES.enter_context(nc.sbuf_tensor("av_t", [128, J], F32))
    Y = ES.enter_context(nc.sbuf_tensor("y_t", [128, J * TC], F32))
    SST = ES.enter_context(nc.sbuf_tensor("sst_t", [128, J], F32))
    VH = ES.enter_context(nc.sbuf_tensor("vh_t", [128, 2 * J], F32))
    SALL = ES.enter_context(nc.sbuf_tensor("sall_t", [128, NCORE * J], F32))
    PIN = ES.enter_context(nc.sbuf_tensor("pin_t", [128, J * NCORE], F32))
    HIN = ES.enter_context(nc.sbuf_tensor("hin_t", [128, J], F32))
    RR_ = ES.enter_context(nc.sbuf_tensor("rr_t", [128, J], F32))
    COEF = ES.enter_context(nc.sbuf_tensor("coef_t", [128, J], F32))
    A512 = ES.enter_context(nc.sbuf_tensor("a512_t", [128, J], F32))
    A2K = ES.enter_context(nc.sbuf_tensor("a2k_t", [128, J], F32))
    ST1R = ES.enter_context(nc.sbuf_tensor("st1r_t", [128, NST * 4], F32))
    ST1M = ES.enter_context(nc.sbuf_tensor("st1m_t", [128, NST * 4], F32))
    ACC1 = ES.enter_context(nc.sbuf_tensor("acc1_t", [128, 8], F32))
    ACC2 = ES.enter_context(nc.sbuf_tensor("acc2_t", [128, 8], F32))
    AS1 = ES.enter_context(nc.sbuf_tensor("as1_t", [128, 256], F32))
    THRW = ES.enter_context(nc.sbuf_tensor("thrw_t", [128, 4], F32))
    STAT = ES.enter_context(nc.sbuf_tensor("stat_t", [128, 16], F32))
    STK = ES.enter_context(nc.sbuf_tensor("stk_t", [128, 8], F32))
    STKT = ES.enter_context(nc.sbuf_tensor("stkt_t", [8, 128], F32))
    DUMP = ES.enter_context(nc.sbuf_tensor("dump_t", [128, 1024], F16))
    XPH = ES.enter_context(nc.sbuf_tensor("xph_t", [128, J * L], F16))
    XPL = ES.enter_context(nc.sbuf_tensor("xpl_t", [128, J * L], F16))
    XHH = ES.enter_context(nc.sbuf_tensor("xhh_t", [128, 2 * J], F16))
    XHL = ES.enter_context(nc.sbuf_tensor("xhl_t", [128, 2 * J], F16))
    XT = ES.enter_context(nc.sbuf_tensor("xt_t", [128, J * L], F32))
    SCA = ES.enter_context(nc.sbuf_tensor("sca_t", [128, J * L], F32))
    SCB = ES.enter_context(nc.sbuf_tensor("scb_t", [128, J * L], F32))
    SCR = ES.enter_context(nc.sbuf_tensor("scr_t", [128, 2 * L], F32))

    PS_T = ES.enter_context(nc.psum_tensor("ps_t", [128, 2048], F32))
    PS_BC = ES.enter_context(nc.psum_tensor("ps_bc", [128, 1024], F32))
    PS_MM = ES.enter_context(nc.psum_tensor("ps_mm", [128, 1024], F32))

    block = ES.enter_context(nc.Block())

    _DRAIN_OPS = frozenset((
        "tensor_copy", "tensor_tensor", "tensor_add", "tensor_sub", "tensor_mul",
        "tensor_max", "tensor_scalar", "tensor_scalar_add", "tensor_scalar_mul",
        "tensor_single_scalar", "scalar_tensor_tensor", "tensor_tensor_scan",
        "tensor_reduce", "reduce_sum", "reduce_max", "reciprocal", "select",
        "activation", "copy", "add", "mul", "bn_stats", "bn_aggr", "transpose",
        "iota"))
    _dw_state = {}

    def _ap_rng(a):
        try:
            t = a.tensor.name
            lo = int(a.offset)
            span = 1
            for st_, ct_ in list(a.ap)[1:]:
                span += abs(int(st_)) * (int(ct_) - 1)
            return (t, lo, lo + span)
        except Exception:
            return None

    def _overlap(r1, r2):
        return r1[0] == r2[0] and r1[1] < r2[2] and r2[1] < r1[2]

    class _DW:
        def __init__(self, e, st):
            object.__setattr__(self, "_e", e)
            object.__setattr__(self, "_st", st)
        def __getattr__(self, n):
            e = object.__getattribute__(self, "_e")
            st = object.__getattribute__(self, "_st")
            a = getattr(e, n)
            if n == "engine_nop":
                def enop(*args, **kw):
                    e.drain()
                    st["r"].clear(); st["w"].clear()
                    return a(*args, **kw)
                return enop
            if n in _DRAIN_OPS:
                def call(*args, **kw):
                    wr, rd = [], []
                    if n == "tensor_copy" and args:
                        wr.append(_ap_rng(args[0]))
                        rd.extend(_ap_rng(x) for x in args[1:])
                    else:
                        rd.extend(_ap_rng(x) for x in args)
                    for k, v_ in kw.items():
                        r = _ap_rng(v_)
                        if r is None:
                            continue
                        (wr if k in ("out", "accum_out") else rd).append(r)
                    wr = [x for x in wr if x is not None]
                    rd = [x for x in rd if x is not None]
                    conflict = (len(st["w"]) + len(st["r"]) > 48)
                    if not conflict:
                        for x in wr:
                            if any(_overlap(x, y) for y in st["w"]) or                                any(_overlap(x, y) for y in st["r"]):
                                conflict = True
                                break
                        if not conflict:
                            for x in rd:
                                if any(_overlap(x, y) for y in st["w"]):
                                    conflict = True
                                    break
                    if conflict:
                        e.drain()
                        st["r"].clear(); st["w"].clear()
                    st["w"].extend(wr)
                    st["r"].extend(rd)
                    return a(*args, **kw)
                return call
            return a

    def bvec(f):
        st = _dw_state.setdefault("v", {"r": [], "w": []})
        block.vector(lambda v: f(_DW(v, st)))

    def bsca(f):
        st = _dw_state.setdefault("s", {"r": [], "w": []})
        block.scalar(lambda s: f(_DW(s, st)))

    def bgps(f):
        st = _dw_state.setdefault("g", {"r": [], "w": []})
        block.gpsimd(lambda gp: f(_DW(gp, st)))

    def raw_transpose(t, out_ap, in_ap, first):
        return t.matmul(out_ap, in_ap, IDT[0:in_ap.shape[0], 0:in_ap.shape[0]],
                        start=first, stop=True, is_transpose=True, skip_group_check=True)

    def sel_mm(t, out_ap, row, n, first):
        return t.matmul(out_ap, SELE[0:8, row * 128:row * 128 + out_ap.shape[0]],
                        STKT[0:8, 0:n], start=first, stop=True, skip_group_check=True)

    def barrier():
        def eV(v): inc(v.engine_nop(), "eV")
        def eG(gp): inc(gp.engine_nop(), "eG")
        bvec(eV); bgps(eG)
        tgt = dict(cnt)
        def wT(t): [wg(t, k, tgt[k]) for k in ("eV", "eS", "eG")]
        def wV(v): [wg(v, k, tgt[k]) for k in ("eT", "eS", "eG")]
        def wS(s): [wg(s, k, tgt[k]) for k in ("eT", "eV", "eG")]
        def wG(gp): [wg(gp, k, tgt[k]) for k in ("eT", "eV", "eS")]
        def wQ(q): [wg(q, k, tgt[k]) for k in ("eT", "eV", "eS", "eG")]
        block.tensor(wT); bvec(wV); bsca(wS); block.gpsimd(wG)
        block.sync(wQ)

    last_stats = {"eS": 0, "eV": 0}

    def ln_stats(zj, W, save=None, use_bc=False, wV=None):
        ng = (W + 127) // 128
        gw = [min(128, W - 128 * gi) for gi in range(ng)]
        eS_base = cnt["eS"]
        eS_prior = last_stats["eS"]
        eV_prior = last_stats["eV"]
        for gi in range(ng):
            def eT_g(t, gi=gi):
                if use_bc:
                    wg(t, "eV")
                    if gi == 0:
                        wg(t, "eS", eS_prior)
                    else:
                        wg(t, "eS", eS_base + gi)
                elif gi < 2:
                    wg(t, "eS", eS_prior)
                    wg(t, "eV", eV_prior)
                    if wV is not None:
                        wg(t, "eV", wV)
                else:
                    wg(t, "eS", eS_base + (gi - 1))
                slot = 0 if use_bc else (gi % 2) * 1024
                pst = PS_BC if use_bc else PS_T
                for j in range(J):
                    co = slot + j * 128
                    ins = raw_transpose(t, pst[0:gw[gi], co:co + 128],
                                        zj(j)[:, 128 * gi:128 * gi + gw[gi]], (co % 512) == 0)
                inc(ins, "eT")
            block.tensor(eT_g)
            eT_now = cnt["eT"]
            def eS_g(s, gi=gi, eT_now=eT_now):
                wg(s, "eT", eT_now)
                slot = 0 if use_bc else (gi % 2) * 1024
                pst = PS_BC if use_bc else PS_T
                s.activation(out=DUMP[0:gw[gi], 0:1024], in_=pst[0:gw[gi], slot:slot + 1024],
                             func=AF.Copy, accum_out=ACC1[0:gw[gi], gi:gi + 1])
                ins = s.activation(out=DUMP[0:gw[gi], 0:1024], in_=pst[0:gw[gi], slot:slot + 1024],
                                   func=AF.Square, accum_out=ACC2[0:gw[gi], gi:gi + 1])
                inc(ins, "eS")
            bsca(eS_g)
        last_stats["eS"] = cnt["eS"]

        P = max(gw)
        def emit_V1(v):
            wg(v, "eS")
            v.tensor_scalar(out=STAT[0:P, 0:ng], in0=ACC1[0:P, 0:ng],
                            scalar1=1.0 / D, scalar2=None, op0=ALU.mult)
            v.tensor_scalar(out=STAT[0:P, 4:4 + ng], in0=ACC2[0:P, 0:ng],
                            scalar1=1.0 / D, scalar2=EPS, op0=ALU.mult, op1=ALU.add)
            v.tensor_tensor(out=STAT[0:P, 8:8 + ng], in0=STAT[0:P, 0:ng],
                            in1=STAT[0:P, 0:ng], op=ALU.mult)
            ins = v.tensor_tensor(out=STAT[0:P, 8:8 + ng], in0=STAT[0:P, 4:4 + ng],
                                  in1=STAT[0:P, 8:8 + ng], op=ALU.subtract)
            inc(ins, "eV")
        bvec(emit_V1)

        def emit_S2(s):
            wg(s, "eV")
            s.activation(out=STAT[0:P, 12:12 + ng], in_=STAT[0:P, 8:8 + ng], func=AF.Ln)
            ins = s.activation(out=STK[0:P, 0:ng], in_=STAT[0:P, 12:12 + ng],
                               func=AF.Exp, scale=CST[0:P, C_NHALF:C_NHALF + 1])
            inc(ins, "eS")
        bsca(emit_S2)

        def emit_V2(v):
            wg(v, "eS")
            ins = v.tensor_tensor(out=STK[0:P, 4:4 + ng], in0=STAT[0:P, 0:ng],
                                  in1=STK[0:P, 0:ng], op=ALU.mult)
            inc(ins, "eV")
            if save is not None:
                sr, sm, scol = save
                v.tensor_copy(sr[0:P, scol:scol + ng], STK[0:P, 0:ng])
                ins = v.tensor_copy(sm[0:P, scol:scol + ng], STK[0:P, 4:4 + ng])
                inc(ins, "eV")
        bvec(emit_V2)
        ln_bcast(P, ng, gw, use_bc)

    def ln_bcast(P, ng, gw, use_bc=False):
        pst = PS_BC if use_bc else PS_T
        def emit_T2(t):
            wg(t, "eV")
            if use_bc:
                wg(t, "eS")
            ins = raw_transpose(t, pst[0:8, 0:P], STK[0:P, 0:8], True)
            inc(ins, "eT")
        block.tensor(emit_T2)

        def emit_V3(v):
            wg(v, "eT")
            ins = v.tensor_copy(STKT[0:8, 0:P], pst[0:8, 0:P])
            inc(ins, "eV")
        bvec(emit_V3)
        last_stats["eV"] = cnt["eV"]

        def emit_T3(t):
            wg(t, "eV")
            for gi in range(ng):
                co = gi * 128
                ins = sel_mm(t, PS_BC[:, co:co + gw[gi]], gi, gw[gi], (co % 512) == 0)
            for gi in range(ng):
                co = 512 + gi * 128
                ins = sel_mm(t, PS_BC[:, co:co + gw[gi]], 4 + gi, gw[gi], (co % 512) == 0)
            inc(ins, "eT")
        block.tensor(emit_T3)

    def p0_sync(q):
        for nm in ("cw0", "cw1", "cw2", "cb", "la", "sb", "sc", "sd", "ob", "lnw", "lnb", "b2"):
            dinc(q.dma_start(out=g[nm][:], in_=vec_in[nm]), "qx")
        dinc(q.dma_start(out=IPB[:], in_=ipb_d), "qx")
        dinc(q.dma_start(out=B1[:], in_=b1_d), "qx")
        dinc(q.dma_start(out=CST[:], in_=cst_d), "qx")
        dinc(q.dma_start(out=SELE[:], in_=sele_d), "qx")
        dinc(q.dma_start(out=IDT[:], in_=idt_d), "qx")
        dinc(q.dma_start(out=SEL[:], in_=sel_d), "qx")
        dinc(q.dma_start(out=MSK[:], in_=msk_d), "qx")
        dinc(q.dma_start(out=XT[:, 0:J].rearrange("p (j t) -> p j t", j=J),
                         in_=xs[:, 0:1].rearrange("(j p) t -> p j t", p=128),
                         allow_slow_non_contiguous=True), "qx")
        dinc(q.dma_start(out=XT[:, J:2 * J].rearrange("p (j t) -> p j t", j=J),
                         in_=xs[:, TC + 1:TC + 2].rearrange("(j p) t -> p j t", p=128),
                         allow_slow_non_contiguous=True), "qx")
    block.sync(p0_sync)
    q0 = cnt["qx"]

    def p0_s(s):
        wg(s, "qx", q0)
        ins = s.activation(out=AV[:], in_=g["la"][:], func=AF.Sigmoid)
        inc(ins, "eS")
    bsca(p0_s)

    def p0_v(v):
        wg(v, "qx", q0)
        fill(v, SST[:], 0.0)
        fill(v, RR_[:], 1.0)
        fill(v, STK[:], 0.0)
        fill(v, STAT[:], 0.0)
        fill(v, SCR[:, L:2 * L], 0.0)
        wg(v, "eS")
        for j in range(J):
            v.tensor_tensor_scan(out=SCR[:, 0:L],
                                 data0=AV[:, j:j + 1].to_broadcast([128, L]),
                                 data1=SCR[:, L:2 * L], initial=1.0,
                                 op0=ALU.mult, op1=ALU.add)
            v.tensor_copy(A512[:, j:j + 1], SCR[:, L - 1:L])
        v.tensor_tensor(out=A2K[:], in0=A512[:], in1=A512[:], op=ALU.mult)
        ins = v.tensor_tensor(out=A2K[:], in0=A2K[:], in1=A2K[:], op=ALU.mult)
        inc(ins, "eV")
    bvec(p0_v)

    ES2 = ExitStack()
    WVH = ES2.enter_context(nc.sbuf_tensor("wvh_t", [128, J * D], F16))
    WVL = ES2.enter_context(nc.sbuf_tensor("wvl_t", [128, J * D], F16))
    VW = [ES2.enter_context(nc.sbuf_tensor("vw%d_t" % p, [128, J * 514], F32))
          for p in range(2)]

    def wsplit(src_dram, n_chunks, wdt, H, Lo):
        ev0 = cnt["eV"]
        qv = []
        def w_dma(q):
            for k in range(n_chunks):
                wg(q, "eV", ev0 if k < 2 else ev0 + (k - 1))
                key = "qwA" if k % 2 == 0 else "qwB"
                dinc(q.dma_start(out=SCA[:, (k % 2) * wdt:(k % 2) * wdt + wdt],
                                 in_=src_dram(k)), key)
                qv.append((key, cnt[key]))
        block.sync(w_dma)
        def w_split(v):
            for k in range(n_chunks):
                wg(v, qv[k][0], qv[k][1])
                src = SCA[:, (k % 2) * wdt:(k % 2) * wdt + wdt]
                v.tensor_copy(H[:, k * wdt:(k + 1) * wdt], src)
                ins = v.scalar_tensor_tensor(out=Lo[:, k * wdt:(k + 1) * wdt],
                                             in0=H[:, k * wdt:(k + 1) * wdt], scalar=-1.0,
                                             in1=src, op0=ALU.mult, op1=ALU.add)
                inc(ins, "eV")
        bvec(w_split)

    wsplit(lambda k: wi[128 * k:128 * (k + 1), D:2 * D], J, D, WVH, WVL)
    barrier()

    def xh_col(tt, j):
        return XT[:, tt * J + j:tt * J + j + 1]

    def p1_T(t):
        for j in range(J):
            co = j * 128
            ins = raw_transpose(t, PS_T[0:2, co:co + 128], XT[:, j:2 * J:J], (co % 512) == 0)
        inc(ins, "eT")
    block.tensor(p1_T)

    def p1_S(s):
        wg(s, "eT")
        s.activation(out=DUMP[0:2, 0:1024], in_=PS_T[0:2, 0:1024], func=AF.Copy,
                     accum_out=ACC1[0:2, 0:1])
        ins = s.activation(out=DUMP[0:2, 0:1024], in_=PS_T[0:2, 0:1024], func=AF.Square,
                           accum_out=ACC2[0:2, 0:1])
        inc(ins, "eS")
    bsca(p1_S)

    def p1_V(v):
        wg(v, "eS")
        v.tensor_scalar(out=STAT[0:2, 0:1], in0=ACC1[0:2, 0:1], scalar1=1.0 / D,
                        scalar2=None, op0=ALU.mult)
        v.tensor_scalar(out=STAT[0:2, 4:5], in0=ACC2[0:2, 0:1], scalar1=1.0 / D,
                        scalar2=EPS, op0=ALU.mult, op1=ALU.add)
        v.tensor_tensor(out=STAT[0:2, 8:9], in0=STAT[0:2, 0:1], in1=STAT[0:2, 0:1],
                        op=ALU.mult)
        ins = v.tensor_tensor(out=STAT[0:2, 8:9], in0=STAT[0:2, 4:5], in1=STAT[0:2, 8:9],
                              op=ALU.subtract)
        inc(ins, "eV")
    bvec(p1_V)

    def p1_S2(s):
        wg(s, "eV")
        s.activation(out=STAT[0:2, 12:13], in_=STAT[0:2, 8:9], func=AF.Ln)
        ins = s.activation(out=STK[0:2, 0:1], in_=STAT[0:2, 12:13], func=AF.Exp,
                           scale=CST[0:2, C_NHALF:C_NHALF + 1])
        inc(ins, "eS")
    bsca(p1_S2)

    def p1_V2(v):
        wg(v, "eS")
        ins = v.tensor_tensor(out=STK[0:2, 4:5], in0=STAT[0:2, 0:1], in1=STK[0:2, 0:1],
                              op=ALU.mult)
        inc(ins, "eV")
    bvec(p1_V2)
    ln_bcast(2, 1, [2])

    def p1_V3(v):
        wg(v, "eT")
        for j in range(J):
            for tt in range(2):
                c0 = tt * J + j
                v.tensor_tensor(out=SCB[:, c0:c0 + 1], in0=xh_col(tt, j),
                                in1=PS_BC[:, tt:tt + 1], op=ALU.mult)
                v.tensor_tensor(out=SCB[:, c0:c0 + 1], in0=SCB[:, c0:c0 + 1],
                                in1=PS_BC[:, 512 + tt:513 + tt], op=ALU.subtract)
        v.tensor_copy(XHH[:, 0:2 * J], SCB[:, 0:2 * J])
        ins = v.scalar_tensor_tensor(out=XHL[:, 0:2 * J], in0=XHH[:, 0:2 * J], scalar=-1.0,
                                     in1=SCB[:, 0:2 * J], op0=ALU.mult, op1=ALU.add)
        inc(ins, "eV")
    bvec(p1_V3)

    def p1_T2(t):
        wg(t, "eV")
        for o in range(J):
            ps = PS_MM[:, o * 2:o * 2 + 2]
            nmm = 0
            for k in range(J):
                for (wt, xt_) in ((WVH, XHH), (WVL, XHH), (WVH, XHL)):
                    ins = t.matmul(ps, wt[:, k * D + o * 128:k * D + (o + 1) * 128],
                                   xt_[:, k:2 * J:J], start=(nmm == 0), stop=(nmm == 23),
                                   skip_group_check=True)
                    nmm += 1
        inc(ins, "eT")
    block.tensor(p1_T2)

    def p1_S3(s):
        wg(s, "eT")
        for j in range(J):
            ps = PS_MM[:, j * 2:j * 2 + 2]
            ins = s.activation(out=VH[:, 2 * j:2 * j + 2], in_=ps, func=AF.Silu,
                               bias=IPB[:, 8 + j:9 + j], scale=1.0)
            inc(ins, "eS")
    bsca(p1_S3)

    def p1_V4(v):
        wg(v, "eS")
        for j in range(J):
            ins = v.tensor_tensor(out=VH[:, 2 * j:2 * j + 2], in0=VH[:, 2 * j:2 * j + 2],
                                  in1=MSK[:, 0:2], op=ALU.mult)
        inc(ins, "eV")
    bvec(p1_V4)
    barrier()

    def abs_chunk(a):
        if a < 128:
            return w1n[128 * (a // 4):128 * (a // 4 + 1),
                       256 * (a % 4):256 * (a % 4) + 256]
        b = a - 128
        return w2t[128 * (b // 4):128 * (b // 4 + 1), 256 * (b % 4):256 * (b % 4) + 256]

    qab_v = {}
    abs_eS = {}
    ABSQ = ["qaA", "qaB"]
    def emit_abs_batch(a0, n):
        def s_abs(s):
            for a in range(a0, a0 + n):
                key = ABSQ[a % 2]
                if a - 2 >= 0:
                    wg(s, "eS", abs_eS[a - 2])
                dinc(s.dma_start(out=TW4[:, (a % 2) * 256:(a % 2) * 256 + 256],
                                 in_=abs_chunk(a)), key)
                qab_v[a] = (key, cnt[key])
                if a > a0:
                    p = a - 1
                    wg(s, qab_v[p][0], qab_v[p][1])
                    ins = s.activation(out=DUMP[:, (p % 4) * 256:(p % 4) * 256 + 256],
                                       in_=TW4[:, (p % 2) * 256:(p % 2) * 256 + 256],
                                       func=AF.Abs, accum_out=AS1[:, p:p + 1])
                    inc(ins, "eS")
                    abs_eS[p] = cnt["eS"]
            p = a0 + n - 1
            wg(s, qab_v[p][0], qab_v[p][1])
            ins = s.activation(out=DUMP[:, (p % 4) * 256:(p % 4) * 256 + 256],
                               in_=TW4[:, (p % 2) * 256:(p % 2) * 256 + 256],
                               func=AF.Abs, accum_out=AS1[:, p:p + 1])
            inc(ins, "eS")
            abs_eS[p] = cnt["eS"]
        bsca(s_abs)

    def load_x(q, i):
        dinc(q.dma_start(out=XT[:].rearrange("p (j t) -> p j t", j=J),
                         in_=xs[:, 1 + i * L:1 + (i + 1) * L].rearrange("(j p) t -> p j t", p=128)),
             "qx")

    def in_proj_v_phase(i):
        eV_prev = cnt["eV"]
        def p2_lx(q, i=i):
            wg(q, "eG")
            load_x(q, i)
        block.sync(p2_lx)
        xq = cnt["qx"]

        def zj(j):
            return XT[:, j * L:(j + 1) * L]
        def wait_x(t):
            wg(t, "qx", xq)
        block.tensor(wait_x)
        ln_stats(zj, L, save=(ST1R, ST1M, 4 * i))

        eV_ap0 = cnt["eV"]
        def emit_apply(v):
            wg(v, "eT")
            wg(v, "qx", xq)
            for j in range(J):
                v.tensor_tensor(out=zj(j), in0=zj(j), in1=PS_BC[:, 0:L], op=ALU.mult)
                ins = v.tensor_tensor(out=zj(j), in0=zj(j), in1=PS_BC[:, 512:512 + L],
                                      op=ALU.subtract)
                inc(ins, "eV")
        bvec(emit_apply)

        eS_hi0 = cnt["eS"]
        def emit_hi(s):
            for j in range(J):
                wg(s, "eV", eV_ap0 + j + 1)
                ins = s.activation(out=XPH[:, j * L:(j + 1) * L], in_=zj(j), func=AF.Copy)
                inc(ins, "eS")
        bsca(emit_hi)

        eT_pmm = cnt["eT"]
        def emit_split(gp):
            wg(gp, "eT", eT_pmm)
            for j in range(J):
                wg(gp, "eS", eS_hi0 + j + 1)
                ins = gp.tensor_tensor(out=XPL[:, j * L:(j + 1) * L], in0=zj(j),
                                       in1=XPH[:, j * L:(j + 1) * L], op=ALU.subtract)
                inc(ins, "eG")
        bgps(emit_split)

        if debug and i == 0:
            def dbg_xn(v):
                for j in range(J):
                    v.tensor_tensor(out=SCA[:, j * L:(j + 1) * L], in0=zj(j),
                                    in1=PS_BC[:, 0:L], op=ALU.mult)
                    ins = v.tensor_tensor(out=SCA[:, j * L:(j + 1) * L],
                                          in0=SCA[:, j * L:(j + 1) * L],
                                          in1=PS_BC[:, 512:512 + L], op=ALU.subtract)
                inc(ins, "eV")
            bvec(dbg_xn)
            def dbg_xn_dma(q):
                wg(q, "eV")
                dinc(q.dma_start(out=dbg["xn0"].rearrange("(j p) t -> p j t", p=128),
                                 in_=SCA[:].rearrange("p (j t) -> p j t", j=J)), "qs")
            block.sync(dbg_xn_dma)

        eG_split = cnt["eG"]
        eT0 = cnt["eT"]
        eS0 = cnt["eS"]
        def emit_mm(t):
            wg(t, "eG", eG_split)
            wg(t, "eS", eS0)
            for o in range(J):
                if o >= 2:
                    wg(t, "eS", eS0 + (o - 1))
                half = (o % 2) * 512
                ps = PS_MM[:, half:half + L]
                nmm = 0
                for k in range(J):
                    for (wt, xt_) in ((WVH, XPH), (WVL, XPH), (WVH, XPL)):
                        ins = t.matmul(ps, wt[:, k * D + o * 128:k * D + (o + 1) * 128],
                                       xt_[:, k * L:(k + 1) * L],
                                       start=(nmm == 0), stop=(nmm == 23))
                        nmm += 1
                inc(ins, "eT")
        block.tensor(emit_mm)

        def emit_evac(s, i=i):
            wg(s, "eV", eV_prev)
            for o in range(J):
                wg(s, "eT", eT0 + o + 1)
                half = (o % 2) * 512
                ps = PS_MM[:, half:half + L]
                ins = s.activation(out=VW[i % 2][:, o * 514 + 1:o * 514 + 513], in_=ps,
                                   func=AF.Silu, bias=IPB[:, 8 + o:9 + o], scale=1.0)
                inc(ins, "eS")
        bsca(emit_evac)

    def mixer_phase(k):
        vwk = VW[k % 2]
        eG_mg0 = cnt["eG"]
        def emit_mix_g(gp):
            wg(gp, "eS")
            if k == 0:
                gp.tensor_copy(vwk[:, 0:J * 514:514], VH[:, 0:2 * J:2])
            else:
                gp.tensor_copy(vwk[:, 0:J * 514:514], VW[(k - 1) % 2][:, 512:J * 514:514])
            if k == NST - 1:
                ins = gp.tensor_copy(vwk[:, 513:J * 514:514], VH[:, 1:2 * J:2])
            else:
                ins = gp.tensor_copy(vwk[:, 513:J * 514:514], VW[(k + 1) % 2][:, 1:J * 514:514])
            inc(ins, "eG")
        bgps(emit_mix_g)

        eG_fix = cnt["eG"]
        def emit_mix_v(v):
            wg(v, "eG", eG_fix)
            wg(v, "eS")
            def vv(a, b, j):
                return vwk[:, j * 514 + a:j * 514 + b]
            for jp in range(0, J, 2):
                for j in (jp, jp + 1):
                    cbuf = SCR[:, (j % 2) * L:(j % 2) * L + L]
                    v.tensor_scalar(out=cbuf, in0=vv(0, 512, j), scalar1=g["cw0"][:, j:j + 1],
                                    scalar2=g["cb"][:, j:j + 1], op0=ALU.mult, op1=ALU.add)
                for j in (jp, jp + 1):
                    cbuf = SCR[:, (j % 2) * L:(j % 2) * L + L]
                    v.scalar_tensor_tensor(out=cbuf, in0=vv(1, 513, j), scalar=g["cw1"][:, j:j + 1],
                                           in1=cbuf, op0=ALU.mult, op1=ALU.add)
                for j in (jp, jp + 1):
                    cbuf = SCR[:, (j % 2) * L:(j % 2) * L + L]
                    v.scalar_tensor_tensor(out=cbuf, in0=vv(2, 514, j), scalar=g["cw2"][:, j:j + 1],
                                           in1=cbuf, op0=ALU.mult, op1=ALU.add)
                for j in (jp, jp + 1):
                    cbuf = SCR[:, (j % 2) * L:(j % 2) * L + L]
                    v.tensor_scalar(out=SCA[:, j * L:(j + 1) * L], in0=cbuf,
                                    scalar1=g["sb"][:, j:j + 1], scalar2=None, op0=ALU.mult)
                for j in (jp, jp + 1):
                    v.tensor_tensor_scan(out=SCB[:, j * L:(j + 1) * L],
                                         data0=AV[:, j:j + 1].to_broadcast([128, L]),
                                         data1=SCA[:, j * L:(j + 1) * L],
                                         initial=SST[:, j:j + 1], op0=ALU.mult, op1=ALU.add)
                for j in (jp, jp + 1):
                    v.tensor_copy(SST[:, j:j + 1], SCB[:, j * L + L - 1:j * L + L])
                for j in (jp, jp + 1):
                    v.tensor_scalar(out=SCB[:, j * L:(j + 1) * L], in0=SCB[:, j * L:(j + 1) * L],
                                    scalar1=g["sc"][:, j:j + 1], scalar2=None, op0=ALU.mult)
                for j in (jp, jp + 1):
                    yj = Y[:, j * TC + k * L:j * TC + (k + 1) * L]
                    cbuf = SCR[:, (j % 2) * L:(j % 2) * L + L]
                    ins = v.scalar_tensor_tensor(out=yj, in0=cbuf, scalar=g["sd"][:, j:j + 1],
                                                 in1=SCB[:, j * L:(j + 1) * L],
                                                 op0=ALU.mult, op1=ALU.add)
                    inc(ins, "eV")
        bvec(emit_mix_v)

    for i in range(NST):
        in_proj_v_phase(i)
        if i >= 1:
            mixer_phase(i - 1)
    mixer_phase(NST - 1)
    barrier()

    if debug:
        def dbg_p2(q):
            dinc(q.dma_start(out=dbg["yloc"].rearrange("(j p) t -> p j t", p=128),
                             in_=Y[:].rearrange("p (j t) -> p j t", j=J)), "qs")
            dinc(q.dma_start(out=dbg["vw0"].rearrange("(j p) t -> p j t", p=128),
                             in_=VW[1][:].rearrange("p (j t) -> p j t", j=J)), "qs")
        block.sync(dbg_p2)
        barrier()
    ES2.close()

    def p3_g(gp):
        dinc(gp.dma_start(out=cc_in.ap(), in_=SST[:]), "qg")
        wg(gp, "qg")
        ins = gp.collective_compute("AllGather", ALU.bypass, ins=[cc_in.ap()],
                                    outs=[cc_out.ap()], replica_groups=[list(range(NCORE))])
        inc(ins, "cc")
        wg(gp, "cc")
        for c in range(NCORE):
            dinc(gp.dma_start(out=SALL[:, c * J:(c + 1) * J], in_=cc_out.ap()[c]), "qg")
    block.gpsimd(p3_g)

    def p3_v(v):
        wg(v, "qg")
        for j in range(J):
            v.tensor_tensor_scan(out=PIN[:, j * NCORE:(j + 1) * NCORE],
                                 data0=A2K[:, j:j + 1].to_broadcast([128, NCORE]),
                                 data1=SALL[:, j:NCORE * J:J], initial=0.0,
                                 op0=ALU.mult, op1=ALU.add)
            v.tensor_tensor(out=PIN[:, j * NCORE:(j + 1) * NCORE],
                            in0=PIN[:, j * NCORE:(j + 1) * NCORE], in1=SEL[:], op=ALU.mult)
            ins = v.tensor_reduce(out=HIN[:, j:j + 1], in_=PIN[:, j * NCORE:(j + 1) * NCORE],
                                  axis=AXX, op=ALU.add)
            inc(ins, "eV")
    bvec(p3_v)
    if debug:
        def dbg_p3(q):
            wg(q, "eV")
            dinc(q.dma_start(out=dbg["hin"], in_=HIN[:]), "qs")
        block.sync(dbg_p3)
    barrier()

    ES4 = ExitStack()
    TW4 = ES4.enter_context(nc.sbuf_tensor("tw4_t", [128, 2 * 256], F32))
    WUH = ES4.enter_context(nc.sbuf_tensor("wuh_t", [128, J * D], F16))
    WUL = ES4.enter_context(nc.sbuf_tensor("wul_t", [128, J * D], F16))
    WOH = ES4.enter_context(nc.sbuf_tensor("woh_t", [128, J * D], F16))
    WOL = ES4.enter_context(nc.sbuf_tensor("wol_t", [128, J * D], F16))

    wsplit(lambda k: wi[128 * k:128 * (k + 1), 0:D], J, D, WUH, WUL)
    wsplit(lambda k: wo[128 * k:128 * (k + 1), :], J, D, WOH, WOL)
    barrier()

    eV_xn_end = [0]
    q3_x = [0]
    for i in range(NST):
        eV_p4prev = cnt["eV"]
        eS_p4prev = cnt["eS"]
        eT_p4prev = cnt["eT"]
        def p4_lx(q, i=i, w=eV_xn_end[0]):
            wg(q, "eV", w)
            load_x(q, i)
        block.sync(p4_lx)
        xq = cnt["qx"]

        def p4_stk(v, i=i):
            v.tensor_copy(STK[:, 0:4], ST1R[:, 4 * i:4 * i + 4])
            ins = v.tensor_copy(STK[:, 4:8], ST1M[:, 4 * i:4 * i + 4])
            inc(ins, "eV")
        bvec(p4_stk)
        ln_bcast(128, 4, [128, 128, 128, 128])

        eV_xn0 = cnt["eV"]
        def p4_xn(v, i=i):
            wg(v, "eT")
            wg(v, "qx", xq)
            for j in range(J):
                a = SCA[:, j * L:(j + 1) * L]
                v.tensor_tensor(out=a, in0=XT[:, j * L:(j + 1) * L], in1=PS_BC[:, 0:L],
                                op=ALU.mult)
                ins = v.tensor_tensor(out=a, in0=a, in1=PS_BC[:, 512:512 + L],
                                      op=ALU.subtract)
                inc(ins, "eV")
        bvec(p4_xn)
        eV_xn_end[0] = cnt["eV"]

        eS_xh0 = cnt["eS"]
        def p4_xn_hi(s, i=i):
            wg(s, "eT", eT_p4prev)
            for j in range(J):
                wg(s, "eV", eV_xn0 + j + 1)
                ins = s.activation(out=XPH[:, j * L:(j + 1) * L],
                                   in_=SCA[:, j * L:(j + 1) * L], func=AF.Copy)
                inc(ins, "eS")
        bsca(p4_xn_hi)

        def p4_xn_split(gp, i=i):
            wg(gp, "eT", eT_p4prev)
            for j in range(J):
                wg(gp, "eS", eS_xh0 + j + 1)
                ins = gp.tensor_tensor(out=XPL[:, j * L:(j + 1) * L],
                                       in0=SCA[:, j * L:(j + 1) * L],
                                       in1=XPH[:, j * L:(j + 1) * L], op=ALU.subtract)
                inc(ins, "eG")
        bgps(p4_xn_split)

        eG_xn = cnt["eG"]
        eT_u0 = cnt["eT"]
        eS_u0 = cnt["eS"]
        def p4_umm(t, i=i):
            wg(t, "eG", eG_xn)
            wg(t, "eV", eV_p4prev)
            for o in range(J):
                if o >= 2:
                    wg(t, "eS", eS_u0 + (o - 1))
                half = (o % 2) * 512
                ps = PS_MM[:, half:half + L]
                nmm = 0
                for k in range(J):
                    for (wt, xt_) in ((WUH, XPH), (WUL, XPH), (WUH, XPL)):
                        ins = t.matmul(ps, wt[:, k * D + o * 128:k * D + (o + 1) * 128],
                                       xt_[:, k * L:(k + 1) * L],
                                       start=(nmm == 0), stop=(nmm == 23))
                        nmm += 1
                inc(ins, "eT")
        block.tensor(p4_umm)

        def p4_uevac(s, i=i):
            wg(s, "eV", eV_p4prev)
            for o in range(J):
                wg(s, "eT", eT_u0 + o + 1)
                half = (o % 2) * 512
                ps = PS_MM[:, half:half + L]
                ins = s.activation(out=SCB[:, o * L:(o + 1) * L], in_=ps, func=AF.Copy)
                inc(ins, "eS")
        bsca(p4_uevac)

        def p4_corr(v, i=i):
            v.tensor_tensor(out=COEF[:], in0=HIN[:], in1=RR_[:], op=ALU.mult)
            v.tensor_tensor(out=COEF[:], in0=COEF[:], in1=g["sc"][:], op=ALU.mult)
            fill(v, SCR[:, L:2 * L], 0.0)
            for j in range(J):
                v.tensor_tensor_scan(out=SCR[:, 0:L],
                                     data0=AV[:, j:j + 1].to_broadcast([128, L]),
                                     data1=SCR[:, L:2 * L], initial=1.0,
                                     op0=ALU.mult, op1=ALU.add)
                yj = Y[:, j * TC + i * L:j * TC + (i + 1) * L]
                ins = v.scalar_tensor_tensor(out=yj, in0=SCR[:, 0:L],
                                             scalar=COEF[:, j:j + 1], in1=yj,
                                             op0=ALU.mult, op1=ALU.add)
                inc(ins, "eV")
            ins = v.tensor_tensor(out=RR_[:], in0=RR_[:], in1=A512[:], op=ALU.mult)
            inc(ins, "eV")
        bvec(p4_corr)

        eV_corr = cnt["eV"]
        eS_sl0 = cnt["eS"]
        eV_mix_b = cnt["eV"]
        def p4_silu(s, i=i):
            wg(s, "eV", eV_corr)
            for j in range(J):
                if j >= 2:
                    wg(s, "eV", eV_mix_b + (j - 1))
                yj = Y[:, j * TC + i * L:j * TC + (i + 1) * L]
                ins = s.activation(out=SCR[:, (j % 2) * L:(j % 2) * L + L], in_=yj,
                                   func=AF.Silu)
                inc(ins, "eS")
        bsca(p4_silu)

        eT_umm = cnt["eT"]
        def p4_mix(v, i=i):
            wg(v, "eT", eT_umm)
            for j in range(J):
                wg(v, "eS", eS_sl0 + j + 1)
                m = SCB[:, j * L:(j + 1) * L]
                ins = v.scalar_tensor_tensor(out=m, in0=m, scalar=IPB[:, j:j + 1],
                                             in1=SCR[:, (j % 2) * L:(j % 2) * L + L],
                                             op0=ALU.add, op1=ALU.mult)
                inc(ins, "eV")
        bvec(p4_mix)

        eV_mx0 = cnt["eV"] - J
        eS_mh0 = cnt["eS"]
        def p4_mix_hi(s, i=i):
            for j in range(J):
                wg(s, "eV", eV_mx0 + j + 1)
                ins = s.activation(out=XPH[:, j * L:(j + 1) * L],
                                   in_=SCB[:, j * L:(j + 1) * L], func=AF.Copy)
                inc(ins, "eS")
        bsca(p4_mix_hi)

        def p4_mix_split(gp, i=i):
            for j in range(J):
                wg(gp, "eS", eS_mh0 + j + 1)
                m = SCB[:, j * L:(j + 1) * L]
                ins = gp.tensor_tensor(out=XPL[:, j * L:(j + 1) * L], in0=m,
                                       in1=XPH[:, j * L:(j + 1) * L], op=ALU.subtract)
                inc(ins, "eG")
        bgps(p4_mix_split)

        eG_mix = cnt["eG"]
        eT_o0 = cnt["eT"]
        eV_oe0 = cnt["eV"]
        def p4_omm(t, i=i):
            wg(t, "eG", eG_mix)
            for o in range(J):
                if o >= 2:
                    wg(t, "eV", eV_oe0 + (o - 1))
                half = (o % 2) * 512
                ps = PS_MM[:, half:half + L]
                nmm = 0
                for k in range(J):
                    for (wt, xt_) in ((WOH, XPH), (WOL, XPH), (WOH, XPL)):
                        ins = t.matmul(ps, wt[:, k * D + o * 128:k * D + (o + 1) * 128],
                                       xt_[:, k * L:(k + 1) * L],
                                       start=(nmm == 0), stop=(nmm == 23))
                        nmm += 1
                inc(ins, "eT")
        block.tensor(p4_omm)

        def p4_oevac(v, i=i):
            for o in range(J):
                wg(v, "eT", eT_o0 + o + 1)
                half = (o % 2) * 512
                ps = PS_MM[:, half:half + L]
                yj = Y[:, o * TC + i * L:o * TC + (i + 1) * L]
                ins = v.scalar_tensor_tensor(out=yj, in0=ps, scalar=g["ob"][:, o:o + 1],
                                             in1=SCA[:, o * L:(o + 1) * L],
                                             op0=ALU.add, op1=ALU.add)
                inc(ins, "eV")
        bvec(p4_oevac)

        eV_oev = cnt["eV"]
        def p4_x2load(q, i=i):
            wg(q, "eV", eV_oev)
            dinc(q.dma_start(out=SCA[:].rearrange("p (j t) -> p j t", j=J),
                             in_=xs[:, 1 + i * L:1 + (i + 1) * L].rearrange("(j p) t -> p j t", p=128)),
                 "q3A")
            q3_x[0] = cnt["q3A"]
        block.sync(p4_x2load)

        def zj2(j, i=i):
            return Y[:, j * TC + i * L:j * TC + (i + 1) * L]
        ln_stats(zj2, L, wV=cnt["eV"])
        def p4_ln2(v, i=i):
            wg(v, "eT")
            for j in range(J):
                a = SCB[:, j * L:(j + 1) * L]
                v.tensor_tensor(out=a, in0=zj2(j), in1=PS_BC[:, 0:L], op=ALU.mult)
                v.tensor_tensor(out=a, in0=a, in1=PS_BC[:, 512:512 + L], op=ALU.subtract)
                ins = v.tensor_scalar(out=a, in0=a, scalar1=g["lnw"][:, j:j + 1],
                                      scalar2=g["lnb"][:, j:j + 1], op0=ALU.mult,
                                      op1=ALU.add)
                inc(ins, "eV")
        bvec(p4_ln2)

        def zj3(j, i=i):
            return SCB[:, j * L:(j + 1) * L]
        ln_stats(zj3, L, wV=cnt["eV"])
        def p4_ln3(v, i=i, w=None):
            wg(v, "eT")
            wg(v, "q3A", q3_x[0])
            for j in range(J):
                a = SCB[:, j * L:(j + 1) * L]
                yj = Y[:, j * TC + i * L:j * TC + (i + 1) * L]
                v.tensor_tensor(out=a, in0=a, in1=PS_BC[:, 0:L], op=ALU.mult)
                v.tensor_tensor(out=a, in0=a, in1=PS_BC[:, 512:512 + L], op=ALU.subtract)
                ins = v.tensor_tensor(out=yj, in0=a, in1=SCA[:, j * L:(j + 1) * L],
                                      op=ALU.add)
                inc(ins, "eV")
        bvec(p4_ln3)
        emit_abs_batch(*[(0, 86), (86, 86), (172, 68), (240, 16)][i])
    barrier()

    ES4.close()
    if debug:
        def dbg_p4(q):
            dinc(q.dma_start(out=dbg["x2"].rearrange("(j p) t -> p j t", p=128),
                             in_=Y[:].rearrange("p (j t) -> p j t", j=J)), "qs")
        block.sync(dbg_p4)
        barrier()

    EST = ExitStack()
    TW = EST.enter_context(nc.sbuf_tensor("tw_t", [128, 4 * 1024], F32))
    TSC32 = EST.enter_context(nc.sbuf_tensor("tsc32_t", [128, 2 * 1024], F32))
    TSC16 = EST.enter_context(nc.sbuf_tensor("tsc16_t", [128, 2 * 1024], F16))
    TSA16 = EST.enter_context(nc.sbuf_tensor("tsa16_t", [128, 2 * 1024], F16))
    TSB16 = EST.enter_context(nc.sbuf_tensor("tsb16_t", [128, 2 * 1024], F16))
    TOUT = EST.enter_context(nc.sbuf_tensor("tout_t", [128, 2 * 1024], F16))
    W1LQ = ["qmA", "qmB"]
    W1SQ = ["qtA", "qtB"]
    W2SQ = ["qtC", "qtD"]

    def tern_scale_v(v):
        wg(v, "eS")
        v.tensor_reduce(out=STAT[:, 0:1], in_=AS1[:, 0:128], axis=AXX, op=ALU.add)
        v.tensor_reduce(out=STAT[:, 1:2], in_=AS1[:, 128:256], axis=AXX, op=ALU.add)
        ins = v.tensor_copy(AS1[:, 0:2], STAT[:, 0:2])
        inc(ins, "eV")
    bvec(tern_scale_v)

    def tern_scale_T(t):
        wg(t, "eV")
        ins = raw_transpose(t, PS_T[0:2, 0:128], AS1[:, 0:2], True)
        inc(ins, "eT")
    block.tensor(tern_scale_T)

    def tern_scale_v2(v):
        wg(v, "eT")
        v.tensor_reduce(out=STAT[0:2, 0:1], in_=PS_T[0:2, 0:128], axis=AXX, op=ALU.add)
        v.tensor_scalar(out=STAT[0:2, 1:2], in0=STAT[0:2, 0:1], scalar1=1.0 / (D * FF),
                        scalar2=1e-8, op0=ALU.mult, op1=ALU.max)
        v.tensor_scalar(out=STAT[0:2, 2:3], in0=STAT[0:2, 1:2], scalar1=0.5, scalar2=None,
                        op0=ALU.mult)
        ins = v.tensor_scalar(out=STAT[0:2, 3:4], in0=STAT[0:2, 2:3], scalar1=-1.0,
                              scalar2=None, op0=ALU.mult)
        inc(ins, "eV")
    bvec(tern_scale_v2)

    def tern_thr_T(t):
        wg(t, "eV")
        ins = raw_transpose(t, PS_T[0:2, 128:130], STAT[0:2, 2:4], False)
        inc(ins, "eT")
    block.tensor(tern_thr_T)

    def tern_thr_v(v):
        wg(v, "eT")
        ins = v.tensor_copy(STKT[0:2, 0:2], PS_T[0:2, 128:130])
        inc(ins, "eV")
    bvec(tern_thr_v)

    def tern_thr_T2(t):
        wg(t, "eV")
        sel_mm(t, PS_BC[:, 0:2], 0, 2, True)
        ins = sel_mm(t, PS_BC[:, 2:4], 1, 2, False)
        inc(ins, "eT")
    block.tensor(tern_thr_T2)

    def tern_thr_v2(v):
        wg(v, "eT")
        v.tensor_copy(THRW[:, 0:1], PS_BC[:, 0:1])
        v.tensor_copy(THRW[:, 1:2], PS_BC[:, 2:3])
        v.tensor_copy(THRW[:, 2:3], PS_BC[:, 1:2])
        ins = v.tensor_copy(THRW[:, 3:4], PS_BC[:, 3:4])
        inc(ins, "eV")
    bvec(tern_thr_v2)

    L5 = 512
    NS5 = TC // L5
    head_ev = {}

    def p5_head_emit(i):
        def zj4(j, i=i):
            return Y[:, j * TC + i * L5:j * TC + (i + 1) * L5]
        ln_stats(zj4, L5, use_bc=True)
        def p5_xn2(v, i=i):
            wg(v, "eT")
            wg(v, "qs")
            for j in range(J):
                a = SCA[:, j * L5:(j + 1) * L5]
                v.tensor_tensor(out=a, in0=zj4(j), in1=PS_BC[:, 0:L5], op=ALU.mult)
                ins = v.tensor_tensor(out=a, in0=a, in1=PS_BC[:, 512:512 + L5],
                                      op=ALU.subtract)
                inc(ins, "eV")
        bvec(p5_xn2)
        def p5_xn2_split(v, i=i):
            for j in range(J):
                a = SCA[:, j * L5:(j + 1) * L5]
                v.tensor_copy(XPH[:, j * L5:(j + 1) * L5], a)
                ins = v.scalar_tensor_tensor(out=XPL[:, j * L5:(j + 1) * L5],
                                             in0=XPH[:, j * L5:(j + 1) * L5], scalar=-1.0,
                                             in1=a, op0=ALU.mult, op1=ALU.add)
                inc(ins, "eV")
        bvec(p5_xn2_split)
        return cnt["eV"]

    head_ev[0] = p5_head_emit(0)

    qm_v = {}
    qg_v = {}
    qst_w1 = {}
    qst_w2 = {}
    t_eV = {}
    t_eT = {}
    t_eS = {}
    t_eG = {}
    eS_t0 = cnt["eS"]
    for c in range(32):
        def w1_load(s, c=c):
            if c >= 2:
                wg(s, "eV", t_eV[("w1", c - 2)])
            else:
                wg(s, "eS", eS_t0)
            key = W1LQ[c % 2]
            dinc(s.dma_start(out=TW[:, (c % 2) * 1024:(c % 2) * 1024 + 1024],
                             in_=w1n[128 * c:128 * (c + 1), :]), key)
            qm_v[c] = (key, cnt[key])
        block.scalar(w1_load)

        def w1_cmp(v, c=c):
            wg(v, qm_v[c][0], qm_v[c][1])
            if c >= 2:
                wg(v, "eT", t_eT[c - 2])
            tw = TW[:, (c % 2) * 1024:(c % 2) * 1024 + 1024]
            ts32 = TSC32[:, (c % 2) * 1024:(c % 2) * 1024 + 1024]
            v.tensor_scalar(out=ts32, in0=tw, scalar1=THRW[:, 0:1], scalar2=-1.0,
                            op0=ALU.is_gt, op1=ALU.add)
            ins = v.scalar_tensor_tensor(out=ts32, in0=tw, scalar=THRW[:, 1:2],
                                         in1=ts32, op0=ALU.is_gt, op1=ALU.add)
            inc(ins, "eV")
            t_eV[("w1", c)] = cnt["eV"]
        bvec(w1_cmp)

        def w1_tr(t, c=c):
            wg(t, "eV", t_eV[("w1", c)])
            if c >= 2:
                wg(t, "eS", t_eS[c - 2])
            ps = PS_MM[:, 0:1024] if c % 2 == 0 else PS_T[:, 0:1024]
            ts32 = TSC32[:, (c % 2) * 1024:(c % 2) * 1024 + 1024]
            for k in range(8):
                ins = raw_transpose(t, ps[:, 128 * k:128 * (k + 1)],
                                    ts32[:, 128 * k:128 * (k + 1)], (k % 4 == 0))
            inc(ins, "eT")
            t_eT[c] = cnt["eT"]
        block.tensor(w1_tr)

        def w1_evac(s, c=c):
            wg(s, "eT", t_eT[c])
            if c >= 2:
                wg(s, qst_w1[c - 2][0], qst_w1[c - 2][1])
            ps = PS_MM[:, 0:1024] if c % 2 == 0 else PS_T[:, 0:1024]
            ins = s.activation(out=TOUT[:, (c % 2) * 1024:(c % 2) * 1024 + 1024],
                               in_=ps, func=AF.Copy)
            inc(ins, "eS")
            t_eS[c] = cnt["eS"]
        bsca(w1_evac)

        def w1_store(q, c=c):
            wg(q, "eS", t_eS[c])
            key = W1SQ[c % 2]
            dinc(q.dma_start(out=w1b_dram.ap()[c],
                             in_=TOUT[:, (c % 2) * 1024:(c % 2) * 1024 + 1024]), key)
            qst_w1[c] = (key, cnt[key])
        block.sync(w1_store)

        def w2_load(gp, c=c):
            if c >= 2:
                wg(gp, "eV", t_eV[("w2", c - 2)])
            key = "qsA" if c % 2 == 0 else "qsB"
            dinc(gp.dma_start(out=TW[:, (2 + c % 2) * 1024:(2 + c % 2) * 1024 + 1024],
                              in_=w2t[128 * c:128 * (c + 1), :]), key)
            qg_v[c] = (key, cnt[key])
        block.gpsimd(w2_load)

        def w2_cmp(v, c=c):
            wg(v, qg_v[c][0], qg_v[c][1])
            if c >= 2:
                wg(v, "eG", t_eG[c - 2])
            tw = TW[:, (2 + c % 2) * 1024:(2 + c % 2) * 1024 + 1024]
            tsa = TSA16[:, (c % 2) * 1024:(c % 2) * 1024 + 1024]
            tsb = TSB16[:, (c % 2) * 1024:(c % 2) * 1024 + 1024]
            v.tensor_scalar(out=tsa, in0=tw, scalar1=THRW[:, 2:3], scalar2=-1.0,
                            op0=ALU.is_gt, op1=ALU.add)
            ins = v.tensor_scalar(out=tsb, in0=tw, scalar1=THRW[:, 3:4], scalar2=None,
                                  op0=ALU.is_gt)
            inc(ins, "eV")
            t_eV[("w2", c)] = cnt["eV"]
        bvec(w2_cmp)

        def w2_add(gp, c=c):
            wg(gp, "eV", t_eV[("w2", c)])
            if c >= 2:
                wg(gp, qst_w2[c - 2][0], qst_w2[c - 2][1])
            ins = gp.tensor_tensor(out=TSC16[:, (c % 2) * 1024:(c % 2) * 1024 + 1024],
                                   in0=TSA16[:, (c % 2) * 1024:(c % 2) * 1024 + 1024],
                                   in1=TSB16[:, (c % 2) * 1024:(c % 2) * 1024 + 1024],
                                   op=ALU.add)
            inc(ins, "eG")
            t_eG[c] = cnt["eG"]
        bgps(w2_add)

        def w2_store(q, c=c):
            wg(q, "eG", t_eG[c])
            key = W2SQ[c % 2]
            dinc(q.dma_start(out=w2t_dram.ap()[128 * c:128 * (c + 1), :],
                             in_=TSC16[:, (c % 2) * 1024:(c % 2) * 1024 + 1024]), key)
            qst_w2[c] = (key, cnt[key])
        block.sync(w2_store)
    qfin = {k: cnt[k] for k in W1SQ + W2SQ}
    EST.close()

    ES5 = ExitStack()
    HGH = ES5.enter_context(nc.sbuf_tensor("hgh_t", [128, JF * L5], F16))
    HGL = ES5.enter_context(nc.sbuf_tensor("hgl_t", [128, JF * L5], F16))
    W2S = XT[:, 0:2048].bitcast(F16)
    W1SL = XT[:, 2048:4096].bitcast(F16)
    W1Q = ["qwA", "qwB", "q2A", "q2B"]
    W2Q = ["q3A", "q3B", "q3C", "q3D"]

    def p5_ob(o):
        if o < 4:
            return PS_T[:, o * 512:(o + 1) * 512]
        if o < 6:
            return PS_BC[:, (o - 4) * 512:(o - 3) * 512]
        return PS_MM[:, (o - 6) * 512:(o - 5) * 512]

    for i in range(NS5):
        if i > 0:
            head_ev[i] = p5_head_emit(i)
        eG_x2 = None
        eV_x2f = head_ev[i]
        eT_f0 = cnt["eT"]
        eS_g0 = cnt["eS"]
        eV_g0 = cnt["eV"]
        w1q = {}
        eT_pA = cnt["eT"]
        def p5_w1dma(q, i=i):
            for f in range(JF):
                if f >= 4:
                    wg(q, "eT", eT_f0 + (f - 4) + 1)
                elif i > 0:
                    wg(q, "eT", eT_pA)
                if i == 0:
                    wg(q, qst_w1[f][0], qst_w1[f][1])
                key = W1Q[f % 4]
                dinc(q.dma_start(out=W1SL[:, (f % 4) * 1024:(f % 4) * 1024 + 1024],
                                 in_=w1b_dram.ap()[f]), key)
                w1q[f] = (key, cnt[key])
        block.sync(p5_w1dma)
        eT_B0 = eT_f0 + JF
        w2q = {}
        def p5_w2dma(q, i=i):
            for f in range(JF):
                if f >= 4:
                    wg(q, "eT", eT_B0 + (f - 4) + 1)
                elif i > 0:
                    wg(q, "eT", eT_pA)
                if i == 0:
                    wg(q, qst_w2[f][0], qst_w2[f][1])
                key = W2Q[f % 4]
                dinc(q.dma_start(out=W2S[:, (f % 4) * 1024:(f % 4) * 1024 + 1024],
                                 in_=w2t_dram.ap()[128 * f:128 * (f + 1), :]), key)
                w2q[f] = (key, cnt[key])
        block.sync(p5_w2dma)

        def p5_ffnA(t, i=i):
            wg(t, "eV", eV_x2f)
            for f in range(JF):
                if f >= 2:
                    wg(t, "eS", eS_g0 + (f - 1))
                wg(t, w1q[f][0], w1q[f][1])
                half = (f % 2) * 512
                ps = PS_MM[:, half:half + L5]
                nmm = 0
                for k in range(J):
                    for xt_ in (XPH, XPL):
                        ins = t.matmul(ps, W1SL[:, (f % 4) * 1024 + k * 128:(f % 4) * 1024 + (k + 1) * 128],
                                       xt_[:, k * L5:(k + 1) * L5],
                                       start=(nmm == 0), stop=(nmm == 15))
                        nmm += 1
                inc(ins, "eT")
        block.tensor(p5_ffnA)

        eV_sp0 = cnt["eV"]
        def p5_gelu(s, i=i):
            for f in range(JF):
                wg(s, "eT", eT_f0 + f + 1)
                if f >= 2:
                    wg(s, "eV", eV_sp0 + (f - 1))
                half = (f % 2) * 512
                ps = PS_MM[:, half:half + L5]
                ins = s.activation(out=SCR[:, (f % 2) * L:(f % 2) * L + L5], in_=ps,
                                   func=AF.Gelu, bias=B1[:, f:f + 1], scale=1.0)
                inc(ins, "eS")
        bsca(p5_gelu)

        def p5_split(v, i=i):
            for f in range(JF):
                wg(v, "eS", eS_g0 + f + 1)
                if i == 0 and f == 0:
                    for kk in W1SQ + W2SQ:
                        wg(v, kk, qfin[kk])
                src = SCR[:, (f % 2) * L:(f % 2) * L + L5]
                v.tensor_copy(HGH[:, f * L5:(f + 1) * L5], src)
                ins = v.scalar_tensor_tensor(out=HGL[:, f * L5:(f + 1) * L5],
                                             in0=HGH[:, f * L5:(f + 1) * L5], scalar=-1.0,
                                             in1=src, op0=ALU.mult, op1=ALU.add)
                inc(ins, "eV")
        bvec(p5_split)

        def p5_ffnB(t, i=i):
            wg(t, "eV", eV_sp0 + JF)
            for f in range(JF):
                wg(t, w2q[f][0], w2q[f][1])
                for o in range(J):
                    for hi_ in (True, False):
                        ins = t.matmul(p5_ob(o),
                                       W2S[:, (f % 4) * 1024 + o * 128:(f % 4) * 1024 + (o + 1) * 128],
                                       HGH[:, f * L5:(f + 1) * L5] if hi_ else HGL[:, f * L5:(f + 1) * L5],
                                       start=(f == 0 and hi_), stop=(f == JF - 1 and not hi_))
                inc(ins, "eT")
        block.tensor(p5_ffnB)

        eT_all = cnt["eT"]
        def p5_evac(v, i=i):
            wg(v, "eT", eT_all)
            wg(v, "qs")
            for o in range(J):
                ins = v.tensor_scalar(out=SCB[:, o * L5:(o + 1) * L5],
                                      in0=p5_ob(o),
                                      scalar1=g["b2"][:, o:o + 1], scalar2=None,
                                      op0=ALU.add)
                inc(ins, "eV")
        bvec(p5_evac)

        def zj5(j, i=i):
            return SCB[:, j * L5:(j + 1) * L5]
        ln_stats(zj5, L5, use_bc=True)
        def p5_out(v, i=i):
            wg(v, "eT")
            for j in range(J):
                a = SCB[:, j * L5:(j + 1) * L5]
                yj = Y[:, j * TC + i * L5:j * TC + (i + 1) * L5]
                v.tensor_tensor(out=a, in0=a, in1=PS_BC[:, 0:L5], op=ALU.mult)
                v.tensor_tensor(out=a, in0=a, in1=PS_BC[:, 512:512 + L5], op=ALU.subtract)
                ins = v.tensor_tensor(out=a, in0=a, in1=yj, op=ALU.add)
                inc(ins, "eV")
        bvec(p5_out)

        def p5_store(q, i=i):
            wg(q, "eV")
            dinc(q.dma_start(out=out_d[:, i * L5:(i + 1) * L5].rearrange("(j p) t -> p j t", p=128),
                             in_=SCB[:, 0:J * L5].rearrange("p (j t) -> p j t", j=J)), "qs")
        block.sync(p5_store)

    barrier()
    ES5.close()
    qs_fin = cnt["qs"]
    def fin(gp):
        wg(gp, "qs", qs_fin)
    block.gpsimd(fin)

    ES.close()
    return nc


def _vec8(v):
    return np.ascontiguousarray(np.asarray(v, np.float32).reshape(J, 128).T)


def kernel(L0_attn_x, L0_attn_in_proj_w, L0_attn_in_proj_b, L0_attn_conv_w,
           L0_attn_conv_b, L0_attn_logit_a, L0_attn_ssm_b, L0_attn_ssm_c,
           L0_attn_ssm_d, L0_attn_out_proj_w, L0_attn_out_proj_b, L0_attn_ln_w,
           L0_attn_ln_b, L0_ffn_W1, L0_ffn_b1, L0_ffn_W2, L0_ffn_b2):
    x = np.asarray(L0_attn_x, dtype=np.float32)
    xT = np.ascontiguousarray(x[0].T)
    xTp = np.zeros((D, T_ALL + 2), np.float32)
    xTp[:, 1:T_ALL + 1] = xT

    wi_a = np.ascontiguousarray(np.asarray(L0_attn_in_proj_w, np.float32))
    wo_a = np.ascontiguousarray(np.asarray(L0_attn_out_proj_w, np.float32))
    w1n_a = np.ascontiguousarray(np.asarray(L0_ffn_W1, np.float32))
    w2t_a = np.ascontiguousarray(np.asarray(L0_ffn_W2, np.float32).T)

    cw = np.asarray(L0_attn_conv_w, np.float32)
    vecs = dict(
        cw0=_vec8(cw[:, 0, 0]), cw1=_vec8(cw[:, 0, 1]), cw2=_vec8(cw[:, 0, 2]),
        cb=_vec8(L0_attn_conv_b), la=_vec8(L0_attn_logit_a), sb=_vec8(L0_attn_ssm_b),
        sc=_vec8(L0_attn_ssm_c), sd=_vec8(L0_attn_ssm_d), ob=_vec8(L0_attn_out_proj_b),
        lnw=_vec8(L0_attn_ln_w), lnb=_vec8(L0_attn_ln_b), b2=_vec8(L0_ffn_b2),
    )
    ipb = np.ascontiguousarray(np.asarray(L0_attn_in_proj_b, np.float32).reshape(16, 128).T)
    b1 = np.ascontiguousarray(np.asarray(L0_ffn_b1, np.float32).reshape(JF, 128).T)

    cst = np.zeros((128, 8), np.float32)
    cst[:, C_NHALF] = -0.5
    sele = np.zeros((8, 1024), np.float32)
    for r in range(8):
        sele[r, r * 128:(r + 1) * 128] = 1.0
    idt = np.eye(128, dtype=np.float32)

    nc = build(debug=False)
    in_maps = []
    for c in range(NCORE):
        sel = np.zeros((128, J), np.float32)
        if c > 0:
            sel[:, c - 1] = 1.0
        msk = np.ones((128, 2), np.float32)
        if c == 0:
            msk[:, 0] = 0.0
        if c == NCORE - 1:
            msk[:, 1] = 0.0
        im = dict(xs=np.ascontiguousarray(xTp[:, c * TC:c * TC + TC + 2]),
                  wi=wi_a, wo=wo_a, w1n=w1n_a, w2t=w2t_a, ipb=ipb, b1=b1, cst=cst,
                  sele=sele, idt=idt, sel=sel, msk=msk, **vecs)
        in_maps.append(im)

    res = run_bass_kernel_spmd(nc, in_maps, core_ids=list(range(NCORE)))
    outs = [res.results[c]["out"] for c in range(NCORE)]
    full = np.concatenate(outs, axis=1)
    return np.ascontiguousarray(full.T)[None].astype(np.float32)
